# version 09
# baseline (speedup 1.0000x reference)
import math
import numpy as np
import ml_dtypes
import concourse.bass as bass
import concourse.mybir as mybir
from concourse.bass_utils import run_bass_kernel_spmd
from contextlib import ExitStack

F32 = mybir.dt.float32
BF16 = mybir.dt.bfloat16
F16 = mybir.dt.float16
U8 = mybir.dt.uint8
U16 = mybir.dt.uint16
AF = mybir.ActivationFunctionType
ALU = mybir.AluOpType

D = 2048
S = 4096
B = 2
NH = 16
RD = 64
LAT = 512
IFF = 5632
GUO = 2 * IFF
NCORES = 8
QOWN = 1024
SMAX = float(128) ** -0.5
KC = 16

GU_NO = GUO // NCORES
DN_NO = D // NCORES
GU_OC = GU_NO // 128
DN_OC = DN_NO // 128
DN_KC = IFF // 128

DEBUG = False

_CACHE = {}


class STrk:
    def __init__(self, h):
        self.h = h
        self.n = 0


def inc(ins, s, k=1):
    ins.then_inc(s.h, k)
    s.n += k
    return s.n


class DmaQ:

    def __init__(self, nc, es, name, depth=8):
        self.sems = [STrk(es.enter_context(nc.semaphore(f"{name}{i}")))
                     for i in range(depth)]
        self.i = 0

    def group(self, eng):
        s = self.sems[self.i % len(self.sems)]
        self.i += 1
        if s.n:
            eng.wait_ge(s.h, s.n)
        return s

    def totals(self):
        return [(s, s.n) for s in self.sems]


def build_program():
    nc = bass.Bass("TRN2", target_bir_lowering=False,
                   detect_race_conditions=False)

    xT = nc.dram_tensor("xT", [D, S], F32, kind="ExternalInput").ap()
    qwT = nc.dram_tensor("qwT", [D, D], F32, kind="ExternalInput").ap()
    kvaT = nc.dram_tensor("kvaT", [D, LAT], F32, kind="ExternalInput").ap()
    kvbT = nc.dram_tensor("kvbT", [LAT, 4096], F32, kind="ExternalInput").ap()
    kvnw = nc.dram_tensor("kvnw", [LAT, 1], F32, kind="ExternalInput").ap()
    owT = nc.dram_tensor("owT", [D, D], F32, kind="ExternalInput").ap()
    cosb = nc.dram_tensor("cosb", [RD, S], BF16, kind="ExternalInput").ap()
    sinnb = nc.dram_tensor("sinnb", [RD, S], BF16, kind="ExternalInput").ap()
    s32i = nc.dram_tensor("s32i", [RD, RD], BF16, kind="ExternalInput").ap()
    gub = nc.dram_tensor("gub", [D // 2, GU_NO], U8, kind="ExternalInput").ap()
    gus = nc.dram_tensor("gus", [KC, GU_NO], F16, kind="ExternalInput").ap()
    dnb = nc.dram_tensor("dnb", [IFF // 2, DN_NO], U8,
                         kind="ExternalInput").ap()
    dns = nc.dram_tensor("dns", [DN_KC, DN_NO], F16,
                         kind="ExternalInput").ap()
    yT = nc.dram_tensor("yT", [D, QOWN], F32, kind="ExternalOutput").ap()

    dq_gu_in = nc.dram_tensor("dq_gu_in", [GU_OC * 128, KC * 128], BF16,
                              kind="Internal").ap()
    dq_dn_in = nc.dram_tensor("dq_dn_in", [DN_OC * 128, DN_KC * 128], BF16,
                              kind="Internal").ap()
    dq_gu = nc.dram_tensor("dq_gu", [NCORES * GU_OC * 128, KC * 128], BF16,
                           kind="Internal").ap()
    dq_dn = nc.dram_tensor("dq_dn", [NCORES * DN_OC * 128, DN_KC * 128], BF16,
                           kind="Internal").ap()
    ikind = "ExternalOutput" if DEBUG else "Internal"
    xn_sc = nc.dram_tensor("xn_sc", [KC * 128, QOWN], BF16, kind=ikind).ap()
    ax_sc = nc.dram_tensor("ax_sc", [D, QOWN], F32, kind=ikind).ap()
    if DEBUG:
        dbg_gu = nc.dram_tensor("dbg_gu", [NCORES * GU_OC * 128, KC * 128],
                                BF16, kind="ExternalOutput").ap()
        dbg_dn = nc.dram_tensor("dbg_dn", [NCORES * DN_OC * 128, DN_KC * 128],
                                BF16, kind="ExternalOutput").ap()
        dbg_dnsb = nc.dram_tensor("dbg_dnsb", [128, 32 * 512], F32,
                                  kind="ExternalOutput").ap()

    es = ExitStack()

    def sem(n_):
        return STrk(es.enter_context(nc.semaphore(n_)))

    with es:
        pb = [es.enter_context(nc.psum_tensor(f"pb{i}", [128, 512], F32))
              for i in range(8)]
        ones1f = es.enter_context(nc.sbuf_tensor("ones1f", [1, 128], F32))
        ones128 = es.enter_context(nc.sbuf_tensor("ones128", [128, 1], BF16))
        cst = es.enter_context(nc.sbuf_tensor("cst", [128, 2], F32))
        s32_sb = es.enter_context(nc.sbuf_tensor("s32_sb", [RD, RD], BF16))
        s_t = sem("s_t")
        s_v = sem("s_v")
        s_s = sem("s_s")
        s_c = sem("s_c")
        gq = DmaQ(nc, es, "gq")
        yq = DmaQ(nc, es, "yq")
        oq = DmaQ(nc, es, "oq")
        block = es.enter_context(nc.Block())

        def phase_barrier():
            cap = ([(s_t, s_t.n), (s_v, s_v.n), (s_s, s_s.n)]
                   + gq.totals() + yq.totals() + oq.totals())

            def waits(eng):
                for s_, n_ in cap:
                    if n_:
                        eng.wait_ge(s_.h, n_)

            @block.tensor
            def _(tt):
                waits(tt)

            @block.vector
            def _(v):
                waits(v)

            @block.scalar
            def _(s):
                waits(s)

            @block.gpsimd
            def _(g):
                waits(g)

            @block.sync
            def _(sy):
                waits(sy)

        @block.vector
        def _(v):
            v.memset(ones1f[:], 1.0)
            v.memset(ones128[:], 1.0)
            v.memset(cst[:, 0:1], -2.0)
            ins = v.memset(cst[:, 1:2], -6.0)
            inc(ins, s_v)
        n_consts = s_v.n

        s32_g = [None]

        @block.sync
        def _(sy):
            sg = yq.group(sy)
            ins = sy.dma_start(out=s32_sb[:], in_=s32i[:, :])
            inc(ins, sg, 16)
            s32_g[0] = (sg, sg.n)

        p1 = ExitStack()
        if True:
            def sb1(n_, shape, dt):
                return p1.enter_context(nc.sbuf_tensor(n_, shape, dt))
            NO = GU_NO
            btile = [sb1(f"dq_bt{i}", [128, NO], U8) for i in range(2)]
            sctile = [sb1(f"dq_sc{i}", [128, NO], F16) for i in range(2)]
            li = sb1("dq_li", [128, NO // 2], U16)
            hi = sb1("dq_hi", [128, NO // 2], U16)
            nf = sb1("dq_nf", [128, NO], BF16)
            sB = sb1("dq_s", [128, NO], BF16)
            mB = sb1("dq_m", [128, NO], BF16)
            aT = sb1("dq_a", [128, NO], BF16)
            bT = sb1("dq_b", [128, NO], BF16)
            d1 = aT
            mag = nf
            wt = [sb1(f"dq_w{i}", [128, NO], BF16) for i in range(2)]

            st_bt = [0, 0]
            st_sc = [0, 0]
            st_wt = [None] * 2
            wslot = [0]

            def emit_dq_block(i, no, bsrc, ssrc, dst):
                if True:
                    bt_ = btile[i % 2]
                    sc_ = sctile[i % 2]
                    bt_g = [None]
                    sc_g = [None]

                    @block.sync
                    def _(sy, i=i, bt_=bt_, bt_g=bt_g):
                        if st_bt[i % 2]:
                            sy.wait_ge(s_v.h, st_bt[i % 2])
                        sg = yq.group(sy)
                        for ap_, c0, cn in bsrc(i):
                            ins = sy.dma_start(out=bt_[:, c0:c0 + cn],
                                               in_=ap_)
                            inc(ins, sg, 16)
                        bt_g[0] = (sg, sg.n)

                    @block.gpsimd
                    def _(g, i=i, sc_=sc_, sc_g=sc_g):
                        if st_sc[i % 2]:
                            g.wait_ge(s_v.h, st_sc[i % 2])
                        sg = gq.group(g)
                        for ap_, p0, pn, c0, cn in ssrc(i):
                            ins = g.dma_start(
                                out=sc_[p0:p0 + pn, c0:c0 + cn], in_=ap_)
                            inc(ins, sg, 16)
                        sc_g[0] = (sg, sg.n)

                    for pl in range(2):
                        w_ = wt[wslot[0] % 2]
                        plane = li if pl == 0 else hi
                        slot = wslot[0] % 2

                        @block.vector
                        def _(v, i=i, pl=pl, plane=plane, bt_=bt_, no=no,
                              bt_g=bt_g):
                            if pl == 0:
                                v.wait_ge(bt_g[0][0].h, bt_g[0][1])
                                u16v = bt_[:, 0:no].bitcast(U16)
                                v.tensor_scalar(li[:, 0:no // 2], u16v,
                                                0x0F0F, None, ALU.bitwise_and)
                                ins = v.tensor_scalar(
                                    hi[:, 0:no // 2], u16v, 4, 0x0F0F,
                                    ALU.logical_shift_right, ALU.bitwise_and)
                                st_bt[i % 2] = inc(ins, s_v)
                                v.wait_ge(s_v.h, st_bt[i % 2])
                            v.tensor_copy(nf[:, 0:no],
                                          plane[:, 0:no // 2].bitcast(U8))
                            v.tensor_scalar(sB[:, 0:no], nf[:, 0:no], 8.0,
                                            None, ALU.is_ge)
                            ins = v.scalar_tensor_tensor(
                                mB[:, 0:no], sB[:, 0:no], -8.0, nf[:, 0:no],
                                ALU.mult, ALU.add)
                            inc(ins, s_v)
                        n_m = s_v.n

                        @block.scalar
                        def _(s, n_m=n_m, no=no):
                            s.wait_ge(s_v.h, n_m)
                            if n_consts:
                                s.wait_ge(s_v.h, n_consts)
                            s.activation(aT[:, 0:no], mB[:, 0:no], AF.Relu,
                                         bias=cst[:, 0:1], scale=0.5)
                            ins = s.activation(bT[:, 0:no], mB[:, 0:no],
                                               AF.Relu, bias=cst[:, 1:2])
                            inc(ins, s_s)
                        n_relu = s_s.n

                        @block.vector
                        def _(v, w_=w_, n_relu=n_relu, no=no, i=i, sc_=sc_,
                              slot=slot, sc_g=sc_g):
                            v.wait_ge(s_s.h, n_relu)
                            if st_wt[slot]:
                                v.wait_ge(st_wt[slot][0].h, st_wt[slot][1])
                            v.scalar_tensor_tensor(d1[:, 0:no], mB[:, 0:no],
                                                   0.5, aT[:, 0:no],
                                                   ALU.mult, ALU.add)
                            v.tensor_add(mag[:, 0:no], d1[:, 0:no],
                                         bT[:, 0:no])
                            v.tensor_scalar(d1[:, 0:no], sB[:, 0:no], -2.0,
                                            1.0, ALU.mult, ALU.add)
                            v.tensor_mul(mag[:, 0:no], mag[:, 0:no],
                                         d1[:, 0:no])
                            v.wait_ge(sc_g[0][0].h, sc_g[0][1])
                            ins = v.tensor_mul(w_[:, 0:no], mag[:, 0:no],
                                               sc_[:, 0:no])
                            st_sc[i % 2] = inc(ins, s_v)
                        n_w = s_v.n

                        @block.sync
                        def _(sy, pl=pl, i=i, w_=w_, n_w=n_w, slot=slot):
                            sy.wait_ge(s_v.h, n_w)
                            sg = yq.group(sy)
                            for ap_, c0, cn in dst(i, pl):
                                ins = sy.dma_start(out=ap_,
                                                   in_=w_[:, c0:c0 + cn])
                                inc(ins, sg, 16)
                            st_wt[slot] = (sg, sg.n)
                        wslot[0] += 1

            def gu_bsrc(i):
                return [(gub[i * 128:(i + 1) * 128, :], 0, GU_NO)]

            def gu_ssrc(i):
                return [(gus[2 * i:2 * i + 1, :].to_broadcast([64, GU_NO]),
                         0, 64, 0, GU_NO),
                        (gus[2 * i + 1:2 * i + 2, :]
                         .to_broadcast([64, GU_NO]), 64, 64, 0, GU_NO)]

            def gu_dst(i, pl):
                kc = 2 * i + pl
                return [(dq_gu_in[oc * 128:(oc + 1) * 128,
                                  kc * 128:(kc + 1) * 128], oc * 128, 128)
                        for oc in range(GU_OC)]

            def dn_bsrc(i):
                return [(dnb[(2 * i) * 128:(2 * i + 1) * 128, :], 0, DN_NO),
                        (dnb[(2 * i + 1) * 128:(2 * i + 2) * 128, :],
                         DN_NO, DN_NO)]

            def dn_ssrc(i):
                out = []
                for sub in range(2):
                    t = 2 * i + sub
                    for half in range(2):
                        out.append(
                            (dns[2 * t + half:2 * t + half + 1, :]
                             .to_broadcast([64, DN_NO]),
                             64 * half, 64, sub * DN_NO, DN_NO))
                return out

            def dn_dst(i, pl):
                out = []
                for sub in range(2):
                    kc = 2 * (2 * i + sub) + pl
                    for oc in range(DN_OC):
                        out.append((dq_dn_in[oc * 128:(oc + 1) * 128,
                                             kc * 128:(kc + 1) * 128],
                                    sub * DN_NO + oc * 128, 128))
                return out

            def _gu_thunk(i):
                return lambda: emit_dq_block(i, GU_NO, gu_bsrc, gu_ssrc,
                                             gu_dst)

            def _dn_thunk(i):
                return lambda: emit_dq_block(i, 2 * DN_NO, dn_bsrc, dn_ssrc,
                                             dn_dst)

            dq_thunks = ([_gu_thunk(i) for i in range(8)]
                         + [_dn_thunk(i) for i in range(11)])
            DQ_SCHED = {0: [2, 3], 1: [4, 5], 2: [6, 7], 3: [8, 9],
                        4: [10, 11], 5: [12, 13, 14], 6: [15, 16],
                        7: [17, 18]}

        p2o = ExitStack()
        latn = p2o.enter_context(nc.sbuf_tensor("latn", [128, 4 * S], BF16))
        kvb_sb = p2o.enter_context(
            nc.sbuf_tensor("kvb_sb", [128, 4 * 4096], BF16))
        O_all = p2o.enter_context(
            nc.sbuf_tensor("O_all", [128, NH * QOWN], BF16))
        with p2o:
            p2 = ExitStack()
            with p2:
                def sb2(n_, shape, dt):
                    return p2.enter_context(nc.sbuf_tensor(n_, shape, dt))
                kvn_sb = sb2("kvn_sb", [128, 4], F32)
                kva_sb = sb2("kva_sb", [128, KC * 512], BF16)
                xb = [sb2(f"xb{i}", [128, KC * 512], BF16) for i in range(2)]
                xsq = [sb2(f"xsq{i}", [128, 512], BF16) for i in range(2)]
                lsq = [sb2(f"lsq{i}", [128, 512], BF16) for i in range(2)]
                strow = sb2("strow", [1, 512 * 6], F32)
                tmp = [sb2("lntmp0", [128, 512], F32)] * 2
                xno = [sb2(f"xno{i}", [128, 512], BF16) for i in range(6)]
                rl_sb = sb2("rl_sb", [128, 512], F32)

                kva_g = [None]
                kvb_g = [None]
                xb_g = [None, None]

                dq_thunks[0]()
                dq_thunks[1]()

                def xb_load(g, sg, t, xb_):
                    for grp in range(4):
                        src = xT[grp * 512:(grp + 1) * 512,
                                 t * 512:(t + 1) * 512].rearrange(
                            "(k p) s -> p k s", p=128)
                        ins = g.dma_start(
                            out=xb_[:, grp * 4 * 512:(grp + 1) * 4 * 512]
                            .rearrange("p (k s) -> p k s", k=4), in_=src)
                        inc(ins, sg, 16)

                @block.gpsimd
                def _(g):
                    sg = gq.group(g)
                    xb_load(g, sg, 0, xb[0])
                    xb_g[0] = (sg, sg.n)
                    sg = gq.group(g)
                    xb_load(g, sg, 1, xb[1])
                    xb_g[1] = (sg, sg.n)
                    sg = gq.group(g)
                    ins = g.dma_start(
                        out=kvn_sb[:, :].rearrange("p (c o) -> p c o", o=1),
                        in_=kvnw.rearrange("(c p) o -> p c o", p=128),
                        allow_slow_non_contiguous=True)
                    inc(ins, sg, 16)
                    for grp in range(4):
                        src = kvaT[grp * 512:(grp + 1) * 512, :].rearrange(
                            "(k p) l -> p k l", p=128)
                        ins = g.dma_start(
                            out=kva_sb[:, grp * 4 * 512:(grp + 1) * 4 * 512]
                            .rearrange("p (k l) -> p k l", k=4), in_=src)
                        inc(ins, sg, 16)
                    kva_g[0] = (sg, sg.n)

                xno_g = [(0, None)] * 6
                prev2 = None
                kva_last = [0, 0]
                for t in range(8):
                    xb_ = xb[t % 2]
                    own = t < 2
                    if t >= 2:
                        @block.gpsimd
                        def _(g, t=t, xb_=xb_):
                            g.wait_ge(s_t.h, kva_last[t % 2])
                            sg = gq.group(g)
                            xb_load(g, sg, t, xb_)
                            xb_g[t % 2] = (sg, sg.n)
                    gxb, nxb = xb_g[t % 2]

                    T0, S0, V0 = s_t.n, s_s.n, s_v.n
                    sums = lambda k, T0=T0: T0 + k + 1
                    bc_n = T0 + 17
                    kva = lambda k, T0=T0: T0 + 17 + k + 1
                    lat2 = lambda m, T0=T0: T0 + 33 + m + 1
                    latbc = T0 + 38
                    sq = lambda k, S0=S0: S0 + k + 1
                    sqrt1 = S0 + 17
                    latsq = lambda m, S0=S0: S0 + 17 + m + 1
                    sqrt2 = S0 + 22
                    rl_n = S0 + 23
                    stat = V0 + 1
                    mr_n = V0 + 2
                    xn_n = lambda k, V0=V0: V0 + 2 + k + 1
                    lrc = V0 + 19
                    latn_ev = V0 + 20

                    spill = {}

                    @block.scalar
                    def _(s, t=t, own=own, xb_=xb_, gxb=gxb, nxb=nxb,
                          sums=sums, kva=kva, lat2=lat2, stat=stat,
                          xn_n=xn_n, lrc=lrc, latbc=latbc, spill=spill):
                        for k in range(KC):
                            if k == 0:
                                s.wait_ge(gxb.h, nxb)
                            if k >= 2:
                                s.wait_ge(s_t.h, sums(k - 2))
                            ins = s.activation(
                                xsq[k % 2][:, :],
                                xb_[:, k * 512:(k + 1) * 512], AF.Square)
                            inc(ins, s_s)
                        s.wait_ge(s_v.h, stat)
                        ins = s.activation(strow[:, 1536:2048],
                                           strow[:, 2560:3072], AF.Sqrt)
                        inc(ins, s_s)
                        if own:
                            for k in range(KC):
                                s.wait_ge(s_v.h, xn_n(k))
                                sg = oq.group(s)
                                ins = s.dma_start(
                                    out=xn_sc[k * 128:(k + 1) * 128,
                                              t * 512:(t + 1) * 512],
                                    in_=xno[k % 6][:, :])
                                inc(ins, sg, 16)
                                spill[k] = (sg, sg.n)
                        for m in range(4):
                            if m == 0:
                                s.wait_ge(s_t.h, kva(15))
                            if m >= 2:
                                s.wait_ge(s_t.h, lat2(m - 2))
                            ins = s.activation(lsq[m % 2][:, :],
                                               pb[4 + m][:, :], AF.Square)
                            inc(ins, s_s)
                        s.wait_ge(s_v.h, lrc)
                        ins = s.activation(strow[:, 1536:2048],
                                           strow[:, 2560:3072], AF.Sqrt)
                        inc(ins, s_s)
                        s.wait_ge(s_t.h, latbc)
                        ins = s.copy(rl_sb[:, :], pb[1][:, :])
                        inc(ins, s_s)

                    @block.tensor
                    def _(tt, t=t, xb_=xb_, gxb=gxb, nxb=nxb, sq=sq,
                          latsq=latsq, sqrt2=sqrt2, mr_n=mr_n, xn_n=xn_n,
                          stat=stat, prev2=prev2):
                        for k in range(KC):
                            if k == 0:
                                tt.wait_ge(gxb.h, nxb)
                                tt.wait_ge(s_v.h, n_consts)
                                if prev2:
                                    tt.wait_ge(s_v.h, prev2["latn_ev"])
                            tt.matmul(pb[0][0:1, :], ones128[:],
                                      xb_[:, k * 512:(k + 1) * 512],
                                      start=(k == 0), stop=(k == 15),
                                      skip_group_check=True)
                            tt.wait_ge(s_s.h, sq(k))
                            ins = tt.matmul(pb[1][0:1, :], ones128[:],
                                            xsq[k % 2][:, :],
                                            start=(k == 0), stop=(k == 15),
                                            skip_group_check=True)
                            inc(ins, s_t)
                        tt.wait_ge(s_v.h, mr_n)
                        tt.matmul(pb[2][:, :], ones1f[:],
                                  strow[:, 1536:2048], start=True, stop=True)
                        ins = tt.matmul(pb[3][:, :], ones1f[:],
                                        strow[:, 2048:2560], start=True,
                                        stop=True)
                        inc(ins, s_t)
                        for k in range(KC):
                            tt.wait_ge(s_v.h, xn_n(k))
                            if t == 0 and k == 0:
                                tt.wait_ge(kva_g[0][0].h, kva_g[0][1])
                            for m in range(4):
                                ins = tt.matmul(
                                    pb[4 + m][:, :],
                                    kva_sb[:, k * 512 + m * 128:
                                           k * 512 + (m + 1) * 128],
                                    xno[k % 6][:, :],
                                    start=(k == 0), stop=(k == 15),
                                    skip_group_check=True)
                            inc(ins, s_t)
                        for m in range(4):
                            tt.wait_ge(s_s.h, latsq(m))
                            if m == 0:
                                tt.wait_ge(s_v.h, stat)
                            ins = tt.matmul(pb[0][0:1, :], ones128[:],
                                            lsq[m % 2][:, :],
                                            start=(m == 0), stop=(m == 3),
                                            skip_group_check=True)
                            inc(ins, s_t)
                        tt.wait_ge(s_s.h, sqrt2)
                        ins = tt.matmul(pb[1][:, :], ones1f[:],
                                        strow[:, 1536:2048], start=True,
                                        stop=True)
                        inc(ins, s_t)

                    @block.vector
                    def _(v, t=t, xb_=xb_, sums=sums, sqrt1=sqrt1,
                          bc_n=bc_n, kva=kva, lat2=lat2, rl_n=rl_n,
                          spill=spill, prev2=prev2,
                          xno_g=list(xno_g)):
                        v.wait_ge(s_t.h, sums(15))
                        m_r = strow[:, 0:512]
                        msq = strow[:, 512:1024]
                        var = strow[:, 1024:1536]
                        v.tensor_scalar(m_r, pb[0][0:1, :], 1.0 / D, None,
                                        ALU.mult)
                        v.tensor_mul(msq, m_r, m_r)
                        v.scalar_tensor_tensor(var, pb[1][0:1, :], 1.0 / D,
                                               msq, ALU.mult, ALU.subtract)
                        v.tensor_scalar(var, var, 1e-5, None, ALU.add)
                        ins = v.reciprocal(strow[:, 2560:3072], var)
                        inc(ins, s_v)
                        v.wait_ge(s_s.h, sqrt1)
                        ins = v.tensor_mul(strow[:, 2048:2560],
                                           strow[:, 0:512],
                                           strow[:, 1536:2048])
                        inc(ins, s_v)
                        for k in range(KC):
                            if k == 0:
                                v.wait_ge(s_t.h, bc_n)
                            if k >= 6:
                                v.wait_ge(s_t.h, kva(k - 6))
                                pg = spill.get(k - 6)
                                if pg:
                                    v.wait_ge(pg[0].h, pg[1])
                            else:
                                gt, go = xno_g[k % 6]
                                if gt:
                                    v.wait_ge(s_t.h, gt)
                                if go:
                                    v.wait_ge(go[0].h, go[1])
                            v.tensor_mul(tmp[k % 2][:, :],
                                         xb_[:, k * 512:(k + 1) * 512],
                                         pb[2][:, :])
                            ins = v.tensor_sub(xno[k % 6][:, :],
                                               tmp[k % 2][:, :],
                                               pb[3][:, :])
                            inc(ins, s_v)
                        v.wait_ge(s_t.h, lat2(3))
                        var = strow[:, 1024:1536]
                        v.tensor_scalar(var, pb[0][0:1, :], 1.0 / LAT, None,
                                        ALU.mult)
                        v.tensor_scalar(var, var, 1e-6, None, ALU.add)
                        ins = v.reciprocal(strow[:, 2560:3072], var)
                        inc(ins, s_v)
                        v.wait_ge(s_s.h, rl_n)
                        for m in range(4):
                            ins = v.tensor_mul(
                                latn[:, m * S + t * 512:
                                     m * S + (t + 1) * 512],
                                pb[4 + m][:, :], rl_sb[:, :])
                        inc(ins, s_v)

                    for k in range(KC):
                        xno_g[k % 6] = (kva(k), spill.get(k))
                    kva_last[t % 2] = kva(15)
                    assert s_t.n == T0 + 38
                    assert s_s.n == S0 + 23
                    assert s_v.n == V0 + 20
                    prev2 = {"latn_ev": latn_ev}

                    for j in DQ_SCHED[t]:
                        dq_thunks[j]()
                    if t == 0:
                        @block.gpsimd
                        def _(g):
                            sg = gq.group(g)
                            ins = g.dma_start(
                                out=kvb_sb[:, :].rearrange(
                                    "p (c o) -> p c o", c=4),
                                in_=kvbT.rearrange("(c p) o -> p c o",
                                                   p=128))
                            inc(ins, sg, 16)
                            kvb_g[0] = (sg, sg.n)
                    if t == 1:
                        @block.vector
                        def _(v):
                            v.wait_ge(kva_g[0][0].h, kva_g[0][1])
                            v.wait_ge(kvb_g[0][0].h, kvb_g[0][1])
                            for c in range(4):
                                ins = v.tensor_scalar(
                                    kvb_sb[:, c * 4096:(c + 1) * 4096],
                                    kvb_sb[:, c * 4096:(c + 1) * 4096],
                                    kvn_sb[:, c:c + 1], None, ALU.mult)
                            inc(ins, s_v)

            phase_barrier()

            p3o = ExitStack()
            Q_all = p3o.enter_context(
                nc.sbuf_tensor("Q_all", [128, NH * QOWN], BF16))
            cs_sb = p3o.enter_context(nc.sbuf_tensor("cs_sb", [RD, S], BF16))
            sn_sb = p3o.enter_context(nc.sbuf_tensor("sn_sb", [RD, S], BF16))
            with p3o:
                p3 = ExitStack()
                with p3:
                    def sb3(n_, shape, dt):
                        return p3.enter_context(nc.sbuf_tensor(n_, shape, dt))
                    qpan = [sb3(f"qpan{i}", [128, KC * 256], BF16)
                            for i in range(2)]
                    xnr = [sb3(f"xnr{i}", [128, QOWN], BF16)
                           for i in range(2)]
                    qcrs = sb3("qcrs", [RD, QOWN], BF16)
                    tq = [sb3(f"tq{i}", [RD, QOWN], BF16) for i in range(2)]

                    cs_g = [None]

                    @block.sync
                    def _(sy):
                        sg = yq.group(sy)
                        ins = sy.dma_start(out=cs_sb[:, :], in_=cosb[:, :])
                        inc(ins, sg, 16)
                        ins = sy.dma_start(out=sn_sb[:, :], in_=sinnb[:, :])
                        inc(ins, sg, 16)
                        cs_g[0] = (sg, sg.n)

                    st_qpan = [0, 0]
                    st_xnr = [0, 0]
                    q_rope = [0]
                    for u in range(8):
                        qp = qpan[u % 2]
                        qp_g = [None]

                        @block.gpsimd
                        def _(g, u=u, qp=qp, qp_g=qp_g):
                            if st_qpan[u % 2]:
                                g.wait_ge(s_t.h, st_qpan[u % 2])
                            sg = gq.group(g)
                            for grp in range(4):
                                src = qwT[grp * 512:(grp + 1) * 512,
                                          u * 256:(u + 1) * 256].rearrange(
                                    "(k p) o -> p k o", p=128)
                                ins = g.dma_start(
                                    out=qp[:, grp * 4 * 256:
                                           (grp + 1) * 4 * 256]
                                    .rearrange("p (k o) -> p k o", k=4),
                                    in_=src)
                                inc(ins, sg, 16)
                            qp_g[0] = (sg, sg.n)

                        for k in range(KC):
                            xn_ = xnr[k % 2]
                            xn_g = [None]

                            @block.sync
                            def _(sy, k=k, xn_=xn_, xn_g=xn_g):
                                if st_xnr[k % 2]:
                                    sy.wait_ge(s_t.h, st_xnr[k % 2])
                                sg = yq.group(sy)
                                ins = sy.dma_start(
                                    out=xn_[:, :],
                                    in_=xn_sc[k * 128:(k + 1) * 128, :])
                                inc(ins, sg, 16)
                                xn_g[0] = (sg, sg.n)

                            @block.tensor
                            def _(tt, k=k, u=u, qp=qp, xn_=xn_,
                                  qp_g=qp_g, xn_g=xn_g):
                                tt.wait_ge(qp_g[0][0].h, qp_g[0][1])
                                tt.wait_ge(xn_g[0][0].h, xn_g[0][1])
                                if k == 0 and q_rope[0]:
                                    tt.wait_ge(s_v.h, q_rope[0])
                                for mi in range(2):
                                    for qc in range(2):
                                        ins = tt.matmul(
                                            pb[mi * 2 + qc][:, :],
                                            qp[:, k * 256 + mi * 128:
                                               k * 256 + (mi + 1) * 128],
                                            xn_[:, qc * 512:(qc + 1) * 512],
                                            start=(k == 0), stop=(k == 15),
                                            skip_group_check=True)
                                st_xnr[k % 2] = inc(ins, s_t)
                        n_qmm = s_t.n
                        st_qpan[u % 2] = n_qmm

                        @block.scalar
                        def _(s, u=u, n_qmm=n_qmm):
                            s.wait_ge(s_t.h, n_qmm)
                            for mi in range(2):
                                h = u * 2 + mi
                                for qc in range(2):
                                    ins = s.copy(
                                        Q_all[:, h * QOWN + qc * 512:
                                              h * QOWN + (qc + 1) * 512],
                                        pb[mi * 2 + qc][:, :])
                            inc(ins, s_s)
                        n_qev = s_s.n

                        for mi in range(2):
                            h = u * 2 + mi

                            @block.tensor
                            def _(tt, h=h, mi=mi, n_qev=n_qev):
                                tt.wait_ge(s_s.h, n_qev)
                                if h == 0:
                                    tt.wait_ge(s32_g[0][0].h, s32_g[0][1])
                                for qc in range(2):
                                    ins = tt.matmul(
                                        pb[mi * 2 + qc][0:RD, :], s32_sb[:],
                                        Q_all[0:RD,
                                              h * QOWN + qc * 512:
                                              h * QOWN + (qc + 1) * 512],
                                        start=True, stop=True)
                                inc(ins, s_t)
                            n_qx = s_t.n

                            @block.vector
                            def _(v, h=h, mi=mi, n_qx=n_qx):
                                v.wait_ge(s_t.h, n_qx)
                                if h == 0:
                                    v.wait_ge(cs_g[0][0].h, cs_g[0][1])
                                for qc in range(2):
                                    v.tensor_copy(
                                        qcrs[:, qc * 512:(qc + 1) * 512],
                                        pb[mi * 2 + qc][0:RD, :])
                                hs = slice(h * QOWN, (h + 1) * QOWN)
                                v.tensor_mul(tq[0][:, :], Q_all[0:RD, hs],
                                             cs_sb[:, 0:QOWN])
                                v.tensor_mul(tq[1][:, :], qcrs[:, :],
                                             sn_sb[:, 0:QOWN])
                                ins = v.tensor_add(Q_all[0:RD, hs],
                                                   tq[0][:, :], tq[1][:, :])
                                inc(ins, s_v)
                            q_rope[0] = s_v.n
                    n_qrope = s_v.n

                    @block.gpsimd
                    def _(g):
                        for s_, n_ in yq.totals():
                            if n_:
                                g.wait_ge(s_.h, n_)
                        ins = g.collective_compute(
                            "AllGather", ALU.bypass,
                            replica_groups=[list(range(NCORES))],
                            ins=[dq_gu_in.opt()], outs=[dq_gu.opt()])
                        inc(ins, s_c)
                        ins = g.collective_compute(
                            "AllGather", ALU.bypass,
                            replica_groups=[list(range(NCORES))],
                            ins=[dq_dn_in.opt()], outs=[dq_dn.opt()])
                        inc(ins, s_c)
                    n_ag_done = s_c.n
                phase_barrier()

                p4 = ExitStack()
                with p4:
                    def sb4(n_, shape, dt):
                        return p4.enter_context(nc.sbuf_tensor(n_, shape, dt))
                    K_pre = sb4("K_pre", [128, S], BF16)
                    K_crs = [sb4(f"K_crs{i}", [RD, 512], BF16) for i in range(2)]
                    V_h = sb4("V_h", [128, S], BF16)
                    tk = [sb4(f"tk{i}", [RD, 512], BF16) for i in range(2)]
                    P_t = [sb4(f"P{i}", [128, 512], BF16) for i in range(3)]
                    rs_row = sb4("rs_row", [1, 1024], F32)
                    rb_sb = sb4("rb_sb", [128, 512], F32)

                    prev = None
                    for h in range(NH):
                        T0, S0, V0 = s_t.n, s_s.n, s_v.n
                        kmm = lambda n, T0=T0: T0 + n + 1
                        xmm = lambda n, T0=T0: T0 + 8 + n + 1
                        vmm = lambda g, T0=T0: T0 + 16 + g + 1
                        smm = (lambda qc, kc, T0=T0:
                               T0 + 24 + qc * 65
                               + (1 if kc == 0 else 2 * kc))
                        avs = (lambda qc, kc, T0=T0:
                               T0 + 24 + qc * 65
                               + (2 * kc + 3 if kc <= 30 else 64))
                        rbc = lambda qc, T0=T0: T0 + 24 + qc * 65 + 65
                        kev = lambda n, S0=S0: S0 + n + 1
                        expn = (lambda qc, kc, S0=S0:
                                S0 + 8 + qc * 33 + kc + 1)
                        rbcp = lambda qc, S0=S0: S0 + 8 + qc * 33 + 33
                        xev = lambda n, V0=V0: V0 + n + 1
                        rope_n = V0 + 8
                        vev = lambda g, V0=V0: V0 + 8 + g + 1
                        recip = lambda qc, V0=V0: V0 + 16 + 2 * qc + 1
                        oev = lambda qc, V0=V0: V0 + 16 + 2 * qc + 2

                        @block.tensor
                        def _(tt, h=h, T0=T0, prev=prev, kev=kev, expn=expn,
                              rbcp=rbcp, xev=xev, vev=vev, recip=recip,
                              oev=oev, rope_n=rope_n):
                            if h == 0:
                                tt.wait_ge(s_v.h, n_qrope)
                            for n in range(8):
                                if n >= 2:
                                    tt.wait_ge(s_s.h, kev(n - 2))
                                elif prev:
                                    tt.wait_ge(s_s.h, prev["kev"](6 + n))
                                for c in range(4):
                                    ins = tt.matmul(
                                        pb[n % 2][:, :],
                                        kvb_sb[:, c * 4096 + h * 256:
                                               c * 4096 + h * 256 + 128],
                                        latn[:, c * S + n * 512:
                                             c * S + (n + 1) * 512],
                                        start=(c == 0), stop=(c == 3))
                                inc(ins, s_t)
                            for n in range(8):
                                tt.wait_ge(s_s.h, kev(n))
                                if n >= 2:
                                    tt.wait_ge(s_v.h, xev(n - 2))
                                elif prev:
                                    tt.wait_ge(s_v.h, prev["vev"](6 + n))
                                ins = tt.matmul(
                                    pb[2 + n % 2][0:RD, :], s32_sb[:],
                                    K_pre[0:RD, n * 512:(n + 1) * 512],
                                    start=True, stop=True)
                                inc(ins, s_t)
                            for g8 in range(8):
                                if g8 < 2:
                                    tt.wait_ge(s_v.h, xev(6 + g8))
                                else:
                                    tt.wait_ge(s_v.h, vev(g8 - 2))
                                for j4 in range(4):
                                    j = g8 * 4 + j4
                                    for c in range(4):
                                        ins = tt.matmul(
                                            pb[2 + g8 % 2][:, j4 * 128:
                                                           (j4 + 1) * 128],
                                            latn[:, c * S + j * 128:
                                                 c * S + (j + 1) * 128],
                                            kvb_sb[:,
                                                   c * 4096 + h * 256 + 128:
                                                   c * 4096 + h * 256 + 256],
                                            start=(c == 0), stop=(c == 3),
                                            skip_group_check=True)
                                inc(ins, s_t)
                            for qc in range(2):
                                def emit_s(kc):
                                    if kc >= 2:
                                        tt.wait_ge(s_s.h, expn(qc, kc - 2))
                                    elif kc == 0:
                                        if qc == 1:
                                            tt.wait_ge(s_s.h, rbcp(0))
                                        elif prev:
                                            tt.wait_ge(s_s.h,
                                                       prev["rbcp"](1))
                                    else:
                                        if qc == 1:
                                            tt.wait_ge(s_s.h, expn(0, 31))
                                        elif prev:
                                            tt.wait_ge(s_s.h,
                                                       prev["expn"](1, 31))
                                    if kc == 0 and qc == 0:
                                        tt.wait_ge(s_v.h, rope_n)
                                    ins = tt.matmul(
                                        pb[6 + kc % 2][:, :],
                                        K_pre[:, kc * 128:(kc + 1) * 128],
                                        Q_all[:, h * QOWN + qc * 512:
                                              h * QOWN + (qc + 1) * 512],
                                        start=True, stop=True)
                                    inc(ins, s_t)

                                def emit_av(kc):
                                    tt.wait_ge(s_s.h, expn(qc, kc))
                                    if kc == 0:
                                        tt.wait_ge(s_v.h, vev(7))
                                        if qc == 1:
                                            tt.wait_ge(s_v.h, oev(0))
                                            tt.wait_ge(s_v.h, recip(0))
                                        elif prev:
                                            tt.wait_ge(s_v.h,
                                                       prev["oev"](1))
                                            tt.wait_ge(s_v.h,
                                                       prev["recip"](1))
                                    tt.matmul(
                                        pb[4][:, :],
                                        V_h[:, kc * 128:(kc + 1) * 128],
                                        P_t[kc % 3][:, :],
                                        start=(kc == 0), stop=(kc == 31),
                                        skip_group_check=True)
                                    ins = tt.matmul(
                                        pb[5][0:1, :], ones128[:],
                                        P_t[kc % 3][:, :],
                                        start=(kc == 0), stop=(kc == 31),
                                        skip_group_check=True)
                                    inc(ins, s_t)

                                for kc in range(32):
                                    emit_s(kc)
                                    if kc >= 1:
                                        emit_av(kc - 1)
                                emit_av(31)
                                tt.wait_ge(s_v.h, recip(qc))
                                ins = tt.matmul(
                                    pb[6][:, :], ones1f[:],
                                    rs_row[:, qc * 512:(qc + 1) * 512],
                                    start=True, stop=True)
                                inc(ins, s_t)

                        @block.scalar
                        def _(s, h=h, S0=S0, prev=prev, kmm=kmm, smm=smm,
                              avs=avs, rbc=rbc, oev=oev):
                            for n in range(8):
                                s.wait_ge(s_t.h, kmm(n))
                                if n == 0 and prev:
                                    s.wait_ge(s_t.h, prev["avs"](1, 31))
                                ins = s.copy(
                                    K_pre[:, n * 512:(n + 1) * 512],
                                    pb[n % 2][:, :])
                                inc(ins, s_s)
                            for qc in range(2):
                                for kc in range(32):
                                    s.wait_ge(s_t.h, smm(qc, kc))
                                    if kc >= 3:
                                        s.wait_ge(s_t.h, avs(qc, kc - 3))
                                    elif qc == 1:
                                        s.wait_ge(s_t.h, avs(0, 29 + kc))
                                    elif prev:
                                        s.wait_ge(s_t.h,
                                                  prev["avs"](1, 29 + kc))
                                    ins = s.activation(P_t[kc % 3][:, :],
                                                       pb[6 + kc % 2][:, :],
                                                       AF.Exp, scale=SMAX)
                                    inc(ins, s_s)
                                s.wait_ge(s_t.h, rbc(qc))
                                if qc == 1:
                                    s.wait_ge(s_v.h, oev(0))
                                elif prev:
                                    s.wait_ge(s_v.h, prev["oev"](1))
                                ins = s.copy(rb_sb[:, :], pb[6][:, :])
                                inc(ins, s_s)

                        @block.vector
                        def _(v, h=h, V0=V0, prev=prev, xmm=xmm, vmm=vmm,
                              avs=avs, rbcp=rbcp):
                            for n in range(8):
                                v.wait_ge(s_t.h, xmm(n))
                                c0 = n * 512
                                v.tensor_copy(K_crs[n % 2][:, :],
                                              pb[2 + n % 2][0:RD, :])
                                v.tensor_mul(tk[0][:, :],
                                             K_pre[0:RD, c0:c0 + 512],
                                             cs_sb[:, c0:c0 + 512])
                                v.tensor_mul(tk[1][:, :],
                                             K_crs[n % 2][:, :],
                                             sn_sb[:, c0:c0 + 512])
                                ins = v.tensor_add(
                                    K_pre[0:RD, c0:c0 + 512],
                                    tk[0][:, :], tk[1][:, :])
                                inc(ins, s_v)
                            for g8 in range(8):
                                v.wait_ge(s_t.h, vmm(g8))
                                if g8 == 0 and prev:
                                    v.wait_ge(s_t.h, prev["avs"](1, 31))
                                ins = v.tensor_copy(
                                    V_h[:, g8 * 512:(g8 + 1) * 512],
                                    pb[2 + g8 % 2][:, :])
                                inc(ins, s_v)
                            for qc in range(2):
                                v.wait_ge(s_t.h, avs(qc, 31))
                                ins = v.reciprocal(
                                    rs_row[:, qc * 512:(qc + 1) * 512],
                                    pb[5][0:1, :])
                                inc(ins, s_v)
                                v.wait_ge(s_s.h, rbcp(qc))
                                ins = v.tensor_mul(
                                    O_all[:, h * QOWN + qc * 512:
                                          h * QOWN + (qc + 1) * 512],
                                    pb[4][:, :], rb_sb[:, :])
                                inc(ins, s_v)

                        assert s_t.n == T0 + 154, (s_t.n, T0)
                        assert s_s.n == S0 + 74, (s_s.n, S0)
                        assert s_v.n == V0 + 20, (s_v.n, V0)
                        prev = {"kev": kev, "expn": expn, "rbcp": rbcp,
                                "vev": vev, "recip": recip, "oev": oev,
                                "avs": avs}
                    n_heads = s_v.n
                phase_barrier()

            p5 = ExitStack()
            with p5:
                def sb5(n_, shape, dt):
                    return p5.enter_context(nc.sbuf_tensor(n_, shape, dt))
                opan = [sb5(f"opan{i}", [128, KC * 512], BF16)
                        for i in range(2)]
                xfo = [sb5(f"xfo{i}", [128, 4 * 1024], F32) for i in range(2)]
                axst = [sb5(f"axst{i}", [128, 512], F32) for i in range(4)]

                st_opan = [0, 0]
                st_xfo = [0, 0]
                st_axst = [None] * 4
                o_ev = [0]
                for mg in range(4):
                    op = opan[mg % 2]
                    xf = xfo[mg % 2]
                    op_g = [None]
                    xf_g = [None]

                    @block.gpsimd
                    def _(g, mg=mg, op=op, op_g=op_g):
                        if st_opan[mg % 2]:
                            g.wait_ge(s_t.h, st_opan[mg % 2])
                        sg = gq.group(g)
                        for grp in range(4):
                            src = owT[grp * 512:(grp + 1) * 512,
                                      mg * 512:(mg + 1) * 512].rearrange(
                                "(k p) o -> p k o", p=128)
                            ins = g.dma_start(
                                out=op[:, grp * 4 * 512:(grp + 1) * 4 * 512]
                                .rearrange("p (k o) -> p k o", k=4),
                                in_=src)
                            inc(ins, sg, 16)
                        op_g[0] = (sg, sg.n)

                    @block.scalar
                    def _(s, mg=mg, xf=xf, xf_g=xf_g):
                        if st_xfo[mg % 2]:
                            s.wait_ge(s_v.h, st_xfo[mg % 2])
                        sg = oq.group(s)
                        src = xT[mg * 512:(mg + 1) * 512, 0:QOWN].rearrange(
                            "(f p) s -> p f s", p=128)
                        ins = s.dma_start(
                            out=xf[:, :].rearrange("p (f s) -> p f s", f=4),
                            in_=src)
                        inc(ins, sg, 16)
                        xf_g[0] = (sg, sg.n)

                    @block.tensor
                    def _(tt, mg=mg, op=op, op_g=op_g):
                        tt.wait_ge(op_g[0][0].h, op_g[0][1])
                        if mg == 0:
                            tt.wait_ge(s_v.h, n_heads)
                        if o_ev[0]:
                            tt.wait_ge(s_v.h, o_ev[0])
                        for k in range(KC):
                            for mi in range(4):
                                for qc in range(2):
                                    ins = tt.matmul(
                                        pb[mi * 2 + qc][:, :],
                                        op[:, k * 512 + mi * 128:
                                           k * 512 + (mi + 1) * 128],
                                        O_all[:, k * QOWN + qc * 512:
                                              k * QOWN + (qc + 1) * 512],
                                        start=(k == 0), stop=(k == 15),
                                        skip_group_check=True)
                        inc(ins, s_t)
                    n_omm = s_t.n
                    st_opan[mg % 2] = n_omm

                    for mi in range(4):
                        for qc in range(2):
                            sl = (mi * 2 + qc) % 4

                            @block.vector
                            def _(v, mi=mi, qc=qc, sl=sl, xf=xf,
                                  n_omm=n_omm, xf_g=xf_g):
                                v.wait_ge(s_t.h, n_omm)
                                v.wait_ge(xf_g[0][0].h, xf_g[0][1])
                                if st_axst[sl]:
                                    v.wait_ge(st_axst[sl][0].h,
                                              st_axst[sl][1])
                                ins = v.tensor_add(
                                    axst[sl][:, :], pb[mi * 2 + qc][:, :],
                                    xf[:, mi * 1024 + qc * 512:
                                       mi * 1024 + (qc + 1) * 512])
                                inc(ins, s_v)
                            n_axv = s_v.n
                            st_xfo[mg % 2] = n_axv
                            if mi == 3 and qc == 1:
                                o_ev[0] = n_axv

                            @block.scalar
                            def _(s, mg=mg, mi=mi, qc=qc, sl=sl,
                                  n_axv=n_axv):
                                s.wait_ge(s_v.h, n_axv)
                                sg = oq.group(s)
                                f = mg * 4 + mi
                                ins = s.dma_start(
                                    out=ax_sc[f * 128:(f + 1) * 128,
                                              qc * 512:(qc + 1) * 512],
                                    in_=axst[sl][:, :])
                                inc(ins, sg, 16)
                                st_axst[sl] = (sg, sg.n)
            n_p5_ev = o_ev[0]
        phase_barrier()
        p6o = ExitStack()
        h_sb = p6o.enter_context(
            nc.sbuf_tensor("h_sb", [128, DN_KC * QOWN], BF16))
        with p6o:
            p6 = ExitStack()
            with p6:
                def sb6(n_, shape, dt):
                    return p6.enter_context(nc.sbuf_tensor(n_, shape, dt))
                xn_sb = sb6("xn_sb", [128, KC * QOWN], BF16)
                wg = [sb6(f"wg{i}", [128, 2048], BF16) for i in range(2)]
                wu = [sb6(f"wu{i}", [128, 2048], BF16) for i in range(2)]
                su = [sb6(f"su{i}", [128, 512], BF16) for i in range(2)]

                T0a, S0a, V0a = s_t.n, s_s.n, s_v.n
                gu_n = lambda g: T0a + g + 1
                sil = lambda g, qc: S0a + 2 * g + qc + 1
                hm = lambda g, qc: V0a + 2 * g + qc + 1

                xnl_g = [None]
                wgrp = []

                @block.sync
                def _(sy):
                    sy.wait_ge(s_c.h, n_ag_done)
                    for s_, n_ in oq.totals():
                        if n_:
                            sy.wait_ge(s_.h, n_)
                    sg = yq.group(sy)
                    for grp in range(4):
                        src = xn_sc[grp * 512:(grp + 1) * 512, :].rearrange(
                            "(k p) s -> p k s", p=128)
                        ins = sy.dma_start(
                            out=xn_sb[:, grp * 4 * QOWN:(grp + 1) * 4 * QOWN]
                            .rearrange("p (k s) -> p k s", k=4), in_=src)
                        inc(ins, sg, 16)
                    xnl_g[0] = (sg, sg.n)
                    for g_ in range(DN_KC):
                        rg = (g_ // 11) * GU_OC + (g_ % 11)
                        ru = (4 + g_ // 11) * GU_OC + (g_ % 11)
                        if g_ >= 2:
                            sy.wait_ge(s_t.h, gu_n(g_ - 2))
                        sg = yq.group(sy)
                        ins = sy.dma_start(
                            out=wg[g_ % 2][:, :],
                            in_=dq_gu[rg * 128:(rg + 1) * 128, :])
                        inc(ins, sg, 16)
                        ins = sy.dma_start(
                            out=wu[g_ % 2][:, :],
                            in_=dq_gu[ru * 128:(ru + 1) * 128, :])
                        inc(ins, sg, 16)
                        wgrp.append((sg, sg.n))

                @block.tensor
                def _(tt):
                    tt.wait_ge(xnl_g[0][0].h, xnl_g[0][1])
                    tt.wait_ge(s_v.h, n_p5_ev)
                    for g_ in range(DN_KC):
                        tt.wait_ge(wgrp[g_][0].h, wgrp[g_][1])
                        if g_ >= 2:
                            tt.wait_ge(s_v.h, hm(g_ - 2, 1))
                        bb = (g_ % 2) * 4
                        for k in range(KC):
                            for qc in range(2):
                                tt.matmul(
                                    pb[bb + qc][:, :],
                                    wg[g_ % 2][:, k * 128:(k + 1) * 128],
                                    xn_sb[:, k * QOWN + qc * 512:
                                          k * QOWN + (qc + 1) * 512],
                                    start=(k == 0), stop=(k == 15),
                                    skip_group_check=True)
                                ins = tt.matmul(
                                    pb[bb + 2 + qc][:, :],
                                    wu[g_ % 2][:, k * 128:(k + 1) * 128],
                                    xn_sb[:, k * QOWN + qc * 512:
                                          k * QOWN + (qc + 1) * 512],
                                    start=(k == 0), stop=(k == 15),
                                    skip_group_check=True)
                        inc(ins, s_t)

                @block.scalar
                def _(s):
                    for g_ in range(DN_KC):
                        bb = (g_ % 2) * 4
                        for qc in range(2):
                            s.wait_ge(s_t.h, gu_n(g_))
                            if g_ >= 1:
                                s.wait_ge(s_v.h, hm(g_ - 1, qc))
                            ins = s.activation(su[qc][:, :],
                                               pb[bb + qc][:, :], AF.Silu)
                            inc(ins, s_s)

                @block.vector
                def _(v):
                    for g_ in range(DN_KC):
                        bb = (g_ % 2) * 4
                        for qc in range(2):
                            v.wait_ge(s_s.h, sil(g_, qc))
                            ins = v.tensor_mul(
                                h_sb[:, g_ * QOWN + qc * 512:
                                     g_ * QOWN + (qc + 1) * 512],
                                su[qc][:, :], pb[bb + 2 + qc][:, :])
                            inc(ins, s_v)

                assert s_t.n == T0a + DN_KC
                assert s_s.n == S0a + 2 * DN_KC
                assert s_v.n == V0a + 2 * DN_KC
                n_h_last = s_v.n
            phase_barrier()

            p6 = ExitStack()
            with p6:
                def sb6(n_, shape, dt):
                    return p6.enter_context(nc.sbuf_tensor(n_, shape, dt))
                dn_sb = sb6("dn_sb", [128, 16 * 512], F32)
                wd = [sb6(f"wd{i}", [128, DN_KC * 128], BF16)
                      for i in range(2)]
                dsq = [sb6(f"dsq{i}", [128, 512], BF16) for i in range(2)]
                ry_sb = sb6("ry_sb", [128, 1024], F32)
                axr = [sb6(f"axr{i}", [128, 512], F32) for i in range(2)]
                ost = [sb6(f"ost{i}", [128, 512], F32) for i in range(2)]
                drow = sb6("drow", [1, 1024], F32)

                prevb = None
                for tb in range(2):
                    T0, S0, V0 = s_t.n, s_s.n, s_v.n
                    dn_n = lambda fc, T0=T0: T0 + (1 if fc == 0 else 2 * fc)
                    s2_n = (lambda fc, T0=T0:
                            T0 + (2 * fc + 3 if fc <= 14 else 32))
                    dbc_n = T0 + 33
                    dnev = lambda fc, S0=S0: S0 + fc + 1
                    dri_n = S0 + 17
                    ry_n = S0 + 18
                    vr_n = V0 + 1
                    o1 = lambda fc, V0=V0: V0 + fc + 2

                    wdg = []
                    axg = []
                    ostg = []

                    @block.sync
                    def _(sy, tb=tb, dn_n=dn_n, prevb=prevb):
                        for fc in range(16):
                            rr = (fc // 2) * DN_OC + (fc % 2)
                            if fc >= 2:
                                sy.wait_ge(s_t.h, dn_n(fc - 2))
                            elif prevb:
                                sy.wait_ge(s_t.h, prevb["dn_n"](14 + fc))
                            sg = yq.group(sy)
                            ins = sy.dma_start(
                                out=wd[fc % 2][:, :],
                                in_=dq_dn[rr * 128:(rr + 1) * 128, :])
                            inc(ins, sg, 16)
                            wdg.append((sg, sg.n))

                    @block.tensor
                    def _(tt, tb=tb, dnev=dnev, dri_n=dri_n, prevb=prevb):
                        def emit_dn(fc):
                            tt.wait_ge(wdg[fc][0].h, wdg[fc][1])
                            if fc >= 2:
                                tt.wait_ge(s_s.h, dnev(fc - 2))
                            elif prevb:
                                tt.wait_ge(s_s.h, prevb["dnev"](14 + fc))
                            elif fc == 0:
                                tt.wait_ge(s_v.h, n_h_last)
                            for k in range(DN_KC):
                                ins = tt.matmul(
                                    pb[(fc % 2) * 2][:, :],
                                    wd[fc % 2][:, k * 128:(k + 1) * 128],
                                    h_sb[:, k * QOWN + tb * 512:
                                         k * QOWN + (tb + 1) * 512],
                                    start=(k == 0), stop=(k == DN_KC - 1))
                            inc(ins, s_t)

                        def emit_s2(fc):
                            tt.wait_ge(s_s.h, dnev(fc))
                            if fc == 0 and prevb:
                                tt.wait_ge(s_v.h, prevb["vr_n"])
                            ins = tt.matmul(pb[4][0:1, :], ones128[:],
                                            dsq[fc % 2][:, :],
                                            start=(fc == 0), stop=(fc == 15),
                                            skip_group_check=True)
                            inc(ins, s_t)

                        for fc in range(16):
                            emit_dn(fc)
                            if fc >= 1:
                                emit_s2(fc - 1)
                        emit_s2(15)
                        tt.wait_ge(s_s.h, dri_n)
                        ins = tt.matmul(pb[5][:, :], ones1f[:],
                                        drow[:, 0:512], start=True,
                                        stop=True)
                        inc(ins, s_t)

                    @block.scalar
                    def _(s, tb=tb, dn_n=dn_n, s2_n=s2_n, dbc_n=dbc_n,
                          vr_n=vr_n, o1=o1, prevb=prevb):
                        for fc in range(16):
                            s.wait_ge(s_t.h, dn_n(fc))
                            if fc >= 2:
                                s.wait_ge(s_t.h, s2_n(fc - 2))
                            elif prevb:
                                s.wait_ge(s_t.h, prevb["s2_n"](14 + fc))
                            if prevb:
                                s.wait_ge(s_v.h, prevb["o1"](fc))
                            s.copy(dn_sb[:, fc * 512:(fc + 1) * 512],
                                   pb[(fc % 2) * 2][:, :])
                            ins = s.activation(dsq[fc % 2][:, :],
                                               pb[(fc % 2) * 2][:, :],
                                               AF.Square)
                            inc(ins, s_s)
                        s.wait_ge(s_v.h, vr_n)
                        ins = s.activation(drow[:, 0:512],
                                           drow[:, 512:1024], AF.Sqrt)
                        inc(ins, s_s)
                        s.wait_ge(s_t.h, dbc_n)
                        ins = s.copy(ry_sb[:, tb * 512:(tb + 1) * 512],
                                     pb[5][:, :])
                        inc(ins, s_s)
                        for fc in range(16):
                            if fc >= 2:
                                s.wait_ge(s_v.h, o1(fc - 2))
                            elif prevb:
                                s.wait_ge(s_v.h, prevb["o1"](14 + fc))
                            sg = oq.group(s)
                            ins = s.dma_start(
                                out=axr[fc % 2][:, :],
                                in_=ax_sc[fc * 128:(fc + 1) * 128,
                                          tb * 512:(tb + 1) * 512])
                            inc(ins, sg, 16)
                            axg.append((sg, sg.n))
                            s.wait_ge(s_v.h, o1(fc))
                            sg = oq.group(s)
                            ins = s.dma_start(
                                out=yT[fc * 128:(fc + 1) * 128,
                                       tb * 512:(tb + 1) * 512],
                                in_=ost[fc % 2][:, :])
                            inc(ins, sg, 16)
                            ostg.append((sg, sg.n))

                    @block.vector
                    def _(v, tb=tb, s2_n=s2_n, ry_n=ry_n, prevb=prevb):
                        v.wait_ge(s_t.h, s2_n(15))
                        r0 = drow[:, 0:512]
                        v.tensor_scalar(r0, pb[4][0:1, :], 1.0 / D, None,
                                        ALU.mult)
                        v.tensor_scalar(r0, r0, 1e-6, None, ALU.add)
                        ins = v.reciprocal(drow[:, 512:1024], r0)
                        inc(ins, s_v)
                        for fc in range(16):
                            if fc == 0:
                                v.wait_ge(s_s.h, ry_n)
                            v.wait_ge(axg[fc][0].h, axg[fc][1])
                            if fc >= 2:
                                v.wait_ge(ostg[fc - 2][0].h,
                                          ostg[fc - 2][1])
                            elif prevb:
                                pg = prevb["ostg"][14 + fc]
                                v.wait_ge(pg[0].h, pg[1])
                            v.tensor_mul(ost[fc % 2][:, :],
                                         dn_sb[:, fc * 512:(fc + 1) * 512],
                                         ry_sb[:, tb * 512:
                                               (tb + 1) * 512])
                            ins = v.tensor_add(ost[fc % 2][:, :],
                                               ost[fc % 2][:, :],
                                               axr[fc % 2][:, :])
                            inc(ins, s_v)

                    assert s_t.n == T0 + 33
                    assert s_s.n == S0 + 18
                    assert s_v.n == V0 + 17
                    prevb = {"dn_n": dn_n, "dnev": dnev, "s2_n": s2_n,
                             "vr_n": vr_n, "o1": o1, "ostg": ostg}
        if DEBUG:
            @block.gpsimd
            def _(g):
                g.wait_ge(s_c.h, n_ag_done)
                sg = gq.group(g)
                ins = g.dma_start(out=dbg_gu, in_=dq_gu)
                inc(ins, sg, 16)
                ins = g.dma_start(out=dbg_dn, in_=dq_dn)
                inc(ins, sg, 16)

        @block.sync
        def _(sy):
            for s_, n_ in (gq.totals() + yq.totals() + oq.totals()):
                if n_:
                    sy.wait_ge(s_.h, n_)

        p1.close()

    return nc



def _perm(n_feat):
    p = np.empty(n_feat, np.int64)
    ar = np.arange(128)
    for t in range(n_feat // 256):
        p[256 * t:256 * t + 128] = 256 * t + 2 * ar
        p[256 * t + 128:256 * t + 256] = 256 * t + 2 * ar + 1
    return p


def _yarn_tables():
    ROPE_DIM, MAX_POS, ORIG_POS, BASE = 64, 131072, 8192, 500000.0
    BETA_FAST, BETA_SLOW = 32.0, 1.0
    scaling = MAX_POS / ORIG_POS
    pos_freqs = BASE ** (np.arange(0, ROPE_DIM, 2, dtype=np.float64)
                         / ROPE_DIM)
    extrap = 1.0 / pos_freqs
    interp = 1.0 / (scaling * pos_freqs)
    low = max(math.floor(ROPE_DIM
                         * math.log(ORIG_POS / (BETA_FAST * 2 * math.pi))
                         / (2 * math.log(BASE))), 0)
    high = min(math.ceil(ROPE_DIM
                         * math.log(ORIG_POS / (BETA_SLOW * 2 * math.pi))
                         / (2 * math.log(BASE))), ROPE_DIM - 1)
    i = np.arange(ROPE_DIM // 2, dtype=np.float64)
    smooth = np.clip((i - low) / max(high - low, 1), 0.0, 1.0)
    inv_freq = ((1.0 - smooth) * interp + smooth * extrap).astype(np.float32)
    pos = np.arange(S, dtype=np.float32)
    freqs = pos[:, None] * inv_freq[None, :]
    emb = np.concatenate([freqs, freqs], -1)
    mscale = 0.1 * math.log(scaling) + 1.0
    cos = (np.cos(emb) * mscale).astype(np.float32).T
    sin = (np.sin(emb) * mscale).astype(np.float32).T
    sinneg = sin.copy()
    sinneg[0:32] = -sin[0:32]
    return cos, sinneg


def kernel(L0_attn_hidden_states, L0_attn_q_w, L0_attn_kv_a_w, L0_attn_kv_b_w,
           L0_attn_o_w, L0_attn_kv_norm_w, L0_ffn_gate_up_packed,
           L0_ffn_gate_up_scales, L0_ffn_down_packed, L0_ffn_down_scales):
    x = np.asarray(L0_attn_hidden_states, np.float32)
    q_w = np.asarray(L0_attn_q_w, np.float32)
    kv_a = np.asarray(L0_attn_kv_a_w, np.float32)
    kv_b = np.asarray(L0_attn_kv_b_w, np.float32)
    o_w = np.asarray(L0_attn_o_w, np.float32)
    kv_n = np.asarray(L0_attn_kv_norm_w, np.float32)
    gup = np.asarray(L0_ffn_gate_up_packed, np.int32)
    gusc = np.asarray(L0_ffn_gate_up_scales)
    dnp = np.asarray(L0_ffn_down_packed, np.int32)
    dnsc = np.asarray(L0_ffn_down_scales)

    P_D = _perm(D)
    P_I = _perm(IFF)

    qwT = np.ascontiguousarray(q_w.T[P_D])
    kvaT = np.ascontiguousarray(kv_a[RD:, :].T[P_D])
    kvbT = np.ascontiguousarray(kv_b.T)
    owT = np.ascontiguousarray(o_w.T[:, P_D])
    kvnw = np.ascontiguousarray(kv_n.reshape(LAT, 1))
    cos, sinneg = _yarn_tables()
    s32 = np.zeros((RD, RD), np.float32)
    s32[(np.arange(RD) + 32) % RD, np.arange(RD)] = 1.0
    s32 = s32.astype(ml_dtypes.bfloat16)

    gu_stack = np.concatenate([gup[:IFF][P_I], gup[IFF:][P_I]], 0)
    gubT = np.ascontiguousarray(gu_stack.view(np.uint8).T)
    gus_stack = np.concatenate(
        [gusc[:, :IFF][:, P_I], gusc[:, IFF:][:, P_I]], 1)
    dnbT = np.ascontiguousarray(dnp[P_D].view(np.uint8).T)
    dns_perm = np.ascontiguousarray(dnsc[:, P_D])

    if "nc" not in _CACHE:
        _CACHE["nc"] = build_program()
    nc = _CACHE["nc"]

    in_maps = []
    for c in range(NCORES):
        b, r = c // 4, c % 4
        xTc = np.roll(x[b].T[P_D], -QOWN * r, axis=1)
        in_maps.append({
            "xT": np.ascontiguousarray(xTc),
            "qwT": qwT,
            "kvaT": kvaT,
            "kvbT": kvbT,
            "kvnw": kvnw,
            "owT": owT,
            "cosb": np.ascontiguousarray(
                np.roll(cos, -QOWN * r, axis=1)).astype(ml_dtypes.bfloat16),
            "sinnb": np.ascontiguousarray(
                np.roll(sinneg, -QOWN * r, axis=1)).astype(
                    ml_dtypes.bfloat16),
            "s32i": s32,
            "gub": np.ascontiguousarray(gubT[:, GU_NO * c:GU_NO * (c + 1)]),
            "gus": np.ascontiguousarray(
                gus_stack[:, GU_NO * c:GU_NO * (c + 1)]),
            "dnb": np.ascontiguousarray(dnbT[:, DN_NO * c:DN_NO * (c + 1)]),
            "dns": np.ascontiguousarray(
                dns_perm[:, DN_NO * c:DN_NO * (c + 1)]),
        })

    res = run_bass_kernel_spmd(nc, in_maps, core_ids=list(range(NCORES)))

    out = np.empty((B, S, D), np.float32)
    for c in range(NCORES):
        b, r = c // 4, c % 4
        yTc = np.asarray(res.results[c]["yT"])
        blk = np.empty((QOWN, D), np.float32)
        blk[:, P_D] = yTc.T
        out[b, QOWN * r:QOWN * (r + 1), :] = blk
    if DEBUG:
        _CACHE["debug"] = res
    return out


# version 10 (final)
# speedup vs baseline: 1.0439x; 1.0439x over previous
import math
import numpy as np
import ml_dtypes
import concourse.bass as bass
import concourse.mybir as mybir
from concourse.bass_utils import run_bass_kernel_spmd
from contextlib import ExitStack

F32 = mybir.dt.float32
BF16 = mybir.dt.bfloat16
F16 = mybir.dt.float16
U8 = mybir.dt.uint8
U16 = mybir.dt.uint16
AF = mybir.ActivationFunctionType
ALU = mybir.AluOpType

D = 2048
S = 4096
B = 2
NH = 16
RD = 64
LAT = 512
IFF = 5632
GUO = 2 * IFF
NCORES = 8
QOWN = 1024
SMAX = float(128) ** -0.5
KC = 16

GU_NO = GUO // NCORES
DN_NO = D // NCORES
GU_OC = GU_NO // 128
DN_OC = DN_NO // 128
DN_KC = IFF // 128

DEBUG = False

_CACHE = {}


class STrk:
    def __init__(self, h):
        self.h = h
        self.n = 0


def inc(ins, s, k=1):
    ins.then_inc(s.h, k)
    s.n += k
    return s.n


class DmaQ:

    def __init__(self, nc, es, name, depth=8):
        self.sems = [STrk(es.enter_context(nc.semaphore(f"{name}{i}")))
                     for i in range(depth)]
        self.i = 0

    def group(self, eng):
        s = self.sems[self.i % len(self.sems)]
        self.i += 1
        if s.n:
            eng.wait_ge(s.h, s.n)
        return s

    def totals(self):
        return [(s, s.n) for s in self.sems]


def build_program():
    nc = bass.Bass("TRN2", target_bir_lowering=False,
                   detect_race_conditions=False)

    xT = nc.dram_tensor("xT", [D, S], F32, kind="ExternalInput").ap()
    qwT = nc.dram_tensor("qwT", [D, D], F32, kind="ExternalInput").ap()
    kvaT = nc.dram_tensor("kvaT", [D, LAT], F32, kind="ExternalInput").ap()
    kvbT = nc.dram_tensor("kvbT", [LAT, 4096], F32, kind="ExternalInput").ap()
    kvnw = nc.dram_tensor("kvnw", [LAT, 1], F32, kind="ExternalInput").ap()
    owT = nc.dram_tensor("owT", [D, D], F32, kind="ExternalInput").ap()
    cosb = nc.dram_tensor("cosb", [RD, S], BF16, kind="ExternalInput").ap()
    sinnb = nc.dram_tensor("sinnb", [RD, S], BF16, kind="ExternalInput").ap()
    s32i = nc.dram_tensor("s32i", [RD, RD], BF16, kind="ExternalInput").ap()
    gub = nc.dram_tensor("gub", [D // 2, GU_NO], U8, kind="ExternalInput").ap()
    gus = nc.dram_tensor("gus", [KC, GU_NO], F16, kind="ExternalInput").ap()
    dnb = nc.dram_tensor("dnb", [IFF // 2, DN_NO], U8,
                         kind="ExternalInput").ap()
    dns = nc.dram_tensor("dns", [DN_KC, DN_NO], F16,
                         kind="ExternalInput").ap()
    yT = nc.dram_tensor("yT", [D, QOWN], F32, kind="ExternalOutput").ap()

    dq_gu_in = nc.dram_tensor("dq_gu_in", [GU_OC * 128, KC * 128], BF16,
                              kind="Internal").ap()
    dq_dn_in = nc.dram_tensor("dq_dn_in", [DN_OC * 128, DN_KC * 128], BF16,
                              kind="Internal").ap()
    dq_gu = nc.dram_tensor("dq_gu", [NCORES * GU_OC * 128, KC * 128], BF16,
                           kind="Internal").ap()
    dq_dn = nc.dram_tensor("dq_dn", [NCORES * DN_OC * 128, DN_KC * 128], BF16,
                           kind="Internal").ap()
    ikind = "ExternalOutput" if DEBUG else "Internal"
    xn_sc = nc.dram_tensor("xn_sc", [KC * 128, QOWN], BF16, kind=ikind).ap()
    ax_sc = nc.dram_tensor("ax_sc", [D, QOWN], F32, kind=ikind).ap()
    if DEBUG:
        dbg_gu = nc.dram_tensor("dbg_gu", [NCORES * GU_OC * 128, KC * 128],
                                BF16, kind="ExternalOutput").ap()
        dbg_dn = nc.dram_tensor("dbg_dn", [NCORES * DN_OC * 128, DN_KC * 128],
                                BF16, kind="ExternalOutput").ap()
        dbg_dnsb = nc.dram_tensor("dbg_dnsb", [128, 32 * 512], F32,
                                  kind="ExternalOutput").ap()

    es = ExitStack()

    def sem(n_):
        return STrk(es.enter_context(nc.semaphore(n_)))

    with es:
        pb = [es.enter_context(nc.psum_tensor(f"pb{i}", [128, 512], F32))
              for i in range(8)]
        ones1f = es.enter_context(nc.sbuf_tensor("ones1f", [1, 128], F32))
        ones128 = es.enter_context(nc.sbuf_tensor("ones128", [128, 1], BF16))
        cst = es.enter_context(nc.sbuf_tensor("cst", [128, 2], F32))
        s32_sb = es.enter_context(nc.sbuf_tensor("s32_sb", [RD, RD], BF16))
        s_t = sem("s_t")
        s_v = sem("s_v")
        s_s = sem("s_s")
        s_c = sem("s_c")
        gq = DmaQ(nc, es, "gq")
        yq = DmaQ(nc, es, "yq")
        oq = DmaQ(nc, es, "oq")
        block = es.enter_context(nc.Block())

        def phase_barrier():
            cap = ([(s_t, s_t.n), (s_v, s_v.n), (s_s, s_s.n)]
                   + gq.totals() + yq.totals() + oq.totals())

            def waits(eng):
                for s_, n_ in cap:
                    if n_:
                        eng.wait_ge(s_.h, n_)

            @block.tensor
            def _(tt):
                waits(tt)

            @block.vector
            def _(v):
                waits(v)

            @block.scalar
            def _(s):
                waits(s)

            @block.gpsimd
            def _(g):
                waits(g)

            @block.sync
            def _(sy):
                waits(sy)

        @block.vector
        def _(v):
            v.memset(ones1f[:], 1.0)
            v.memset(ones128[:], 1.0)
            v.memset(cst[:, 0:1], -2.0)
            ins = v.memset(cst[:, 1:2], -6.0)
            inc(ins, s_v)
        n_consts = s_v.n

        s32_g = [None]

        @block.sync
        def _(sy):
            sg = yq.group(sy)
            ins = sy.dma_start(out=s32_sb[:], in_=s32i[:, :])
            inc(ins, sg, 16)
            s32_g[0] = (sg, sg.n)

        p1 = ExitStack()
        if True:
            def sb1(n_, shape, dt):
                return p1.enter_context(nc.sbuf_tensor(n_, shape, dt))
            NO = GU_NO
            btile = [sb1(f"dq_bt{i}", [128, NO], U8) for i in range(2)]
            sctile = [sb1(f"dq_sc{i}", [128, NO], F16) for i in range(2)]
            li = sb1("dq_li", [128, NO // 2], U16)
            hi = sb1("dq_hi", [128, NO // 2], U16)
            nf = sb1("dq_nf", [128, NO], BF16)
            sB = sb1("dq_s", [128, NO], BF16)
            mB = sb1("dq_m", [128, NO], BF16)
            aT = sb1("dq_a", [128, NO], BF16)
            bT = sb1("dq_b", [128, NO], BF16)
            d1 = aT
            mag = nf
            wt = [sb1(f"dq_w{i}", [128, NO], BF16) for i in range(2)]

            st_bt = [0, 0]
            st_sc = [0, 0]
            st_wt = [None] * 2
            wslot = [0]

            def emit_dq_block(i, no, bsrc, ssrc, dst):
                if True:
                    bt_ = btile[i % 2]
                    sc_ = sctile[i % 2]
                    bt_g = [None]
                    sc_g = [None]

                    @block.sync
                    def _(sy, i=i, bt_=bt_, bt_g=bt_g):
                        if st_bt[i % 2]:
                            sy.wait_ge(s_v.h, st_bt[i % 2])
                        sg = yq.group(sy)
                        for ap_, c0, cn in bsrc(i):
                            ins = sy.dma_start(out=bt_[:, c0:c0 + cn],
                                               in_=ap_)
                            inc(ins, sg, 16)
                        bt_g[0] = (sg, sg.n)

                    @block.gpsimd
                    def _(g, i=i, sc_=sc_, sc_g=sc_g):
                        if st_sc[i % 2]:
                            g.wait_ge(s_v.h, st_sc[i % 2])
                        sg = gq.group(g)
                        for ap_, p0, pn, c0, cn in ssrc(i):
                            ins = g.dma_start(
                                out=sc_[p0:p0 + pn, c0:c0 + cn], in_=ap_)
                            inc(ins, sg, 16)
                        sc_g[0] = (sg, sg.n)

                    for pl in range(2):
                        w_ = wt[wslot[0] % 2]
                        plane = li if pl == 0 else hi
                        slot = wslot[0] % 2

                        @block.vector
                        def _(v, i=i, pl=pl, plane=plane, bt_=bt_, no=no,
                              bt_g=bt_g):
                            if pl == 0:
                                v.wait_ge(bt_g[0][0].h, bt_g[0][1])
                                u16v = bt_[:, 0:no].bitcast(U16)
                                v.tensor_scalar(li[:, 0:no // 2], u16v,
                                                0x0F0F, None, ALU.bitwise_and)
                                ins = v.tensor_scalar(
                                    hi[:, 0:no // 2], u16v, 4, 0x0F0F,
                                    ALU.logical_shift_right, ALU.bitwise_and)
                                st_bt[i % 2] = inc(ins, s_v)
                                v.wait_ge(s_v.h, st_bt[i % 2])
                            v.tensor_copy(nf[:, 0:no],
                                          plane[:, 0:no // 2].bitcast(U8))
                            v.tensor_scalar(sB[:, 0:no], nf[:, 0:no], 8.0,
                                            None, ALU.is_ge)
                            ins = v.scalar_tensor_tensor(
                                mB[:, 0:no], sB[:, 0:no], -8.0, nf[:, 0:no],
                                ALU.mult, ALU.add)
                            inc(ins, s_v)
                        n_m = s_v.n

                        @block.scalar
                        def _(s, n_m=n_m, no=no):
                            s.wait_ge(s_v.h, n_m)
                            if n_consts:
                                s.wait_ge(s_v.h, n_consts)
                            s.activation(aT[:, 0:no], mB[:, 0:no], AF.Relu,
                                         bias=cst[:, 0:1], scale=0.5)
                            ins = s.activation(bT[:, 0:no], mB[:, 0:no],
                                               AF.Relu, bias=cst[:, 1:2])
                            inc(ins, s_s)
                        n_relu = s_s.n

                        @block.vector
                        def _(v, w_=w_, n_relu=n_relu, no=no, i=i, sc_=sc_,
                              slot=slot, sc_g=sc_g):
                            v.wait_ge(s_s.h, n_relu)
                            if st_wt[slot]:
                                v.wait_ge(st_wt[slot][0].h, st_wt[slot][1])
                            v.scalar_tensor_tensor(d1[:, 0:no], mB[:, 0:no],
                                                   0.5, aT[:, 0:no],
                                                   ALU.mult, ALU.add)
                            v.tensor_add(mag[:, 0:no], d1[:, 0:no],
                                         bT[:, 0:no])
                            v.tensor_scalar(d1[:, 0:no], sB[:, 0:no], -2.0,
                                            1.0, ALU.mult, ALU.add)
                            v.tensor_mul(mag[:, 0:no], mag[:, 0:no],
                                         d1[:, 0:no])
                            v.wait_ge(sc_g[0][0].h, sc_g[0][1])
                            ins = v.tensor_mul(w_[:, 0:no], mag[:, 0:no],
                                               sc_[:, 0:no])
                            st_sc[i % 2] = inc(ins, s_v)
                        n_w = s_v.n

                        @block.sync
                        def _(sy, pl=pl, i=i, w_=w_, n_w=n_w, slot=slot):
                            sy.wait_ge(s_v.h, n_w)
                            sg = yq.group(sy)
                            for ap_, c0, cn in dst(i, pl):
                                ins = sy.dma_start(out=ap_,
                                                   in_=w_[:, c0:c0 + cn])
                                inc(ins, sg, 16)
                            st_wt[slot] = (sg, sg.n)
                        wslot[0] += 1

            def gu_bsrc(i):
                return [(gub[i * 128:(i + 1) * 128, :], 0, GU_NO)]

            def gu_ssrc(i):
                return [(gus[2 * i:2 * i + 1, :].to_broadcast([64, GU_NO]),
                         0, 64, 0, GU_NO),
                        (gus[2 * i + 1:2 * i + 2, :]
                         .to_broadcast([64, GU_NO]), 64, 64, 0, GU_NO)]

            def gu_dst(i, pl):
                kc = 2 * i + pl
                return [(dq_gu_in[oc * 128:(oc + 1) * 128,
                                  kc * 128:(kc + 1) * 128], oc * 128, 128)
                        for oc in range(GU_OC)]

            def dn_bsrc(i):
                return [(dnb[(2 * i) * 128:(2 * i + 1) * 128, :], 0, DN_NO),
                        (dnb[(2 * i + 1) * 128:(2 * i + 2) * 128, :],
                         DN_NO, DN_NO)]

            def dn_ssrc(i):
                out = []
                for sub in range(2):
                    t = 2 * i + sub
                    for half in range(2):
                        out.append(
                            (dns[2 * t + half:2 * t + half + 1, :]
                             .to_broadcast([64, DN_NO]),
                             64 * half, 64, sub * DN_NO, DN_NO))
                return out

            def dn_dst(i, pl):
                out = []
                for sub in range(2):
                    kc = 2 * (2 * i + sub) + pl
                    for oc in range(DN_OC):
                        out.append((dq_dn_in[oc * 128:(oc + 1) * 128,
                                             kc * 128:(kc + 1) * 128],
                                    sub * DN_NO + oc * 128, 128))
                return out

            def _gu_thunk(i):
                return lambda: emit_dq_block(i, GU_NO, gu_bsrc, gu_ssrc,
                                             gu_dst)

            def _dn_thunk(i):
                return lambda: emit_dq_block(i, 2 * DN_NO, dn_bsrc, dn_ssrc,
                                             dn_dst)

            dq_thunks = ([_gu_thunk(i) for i in range(8)]
                         + [_dn_thunk(i) for i in range(11)])
            DQ_SCHED = {0: [2, 3], 1: [4, 5], 2: [6, 7], 3: [8, 9],
                        4: [10, 11], 5: [12, 13, 14], 6: [15, 16],
                        7: [17, 18]}

        p2o = ExitStack()
        latn = p2o.enter_context(nc.sbuf_tensor("latn", [128, 4 * S], BF16))
        kvb_sb = p2o.enter_context(
            nc.sbuf_tensor("kvb_sb", [128, 4 * 4096], BF16))
        O_all = p2o.enter_context(
            nc.sbuf_tensor("O_all", [128, NH * QOWN], BF16))
        with p2o:
            p2 = ExitStack()
            with p2:
                def sb2(n_, shape, dt):
                    return p2.enter_context(nc.sbuf_tensor(n_, shape, dt))
                kvn_sb = sb2("kvn_sb", [128, 4], F32)
                kva_sb = sb2("kva_sb", [128, KC * 512], BF16)
                xb = [sb2(f"xb{i}", [128, KC * 512], BF16) for i in range(2)]
                xsq = [sb2(f"xsq{i}", [128, 512], BF16) for i in range(2)]
                lsq = [sb2(f"lsq{i}", [128, 512], BF16) for i in range(2)]
                strow = sb2("strow", [1, 512 * 6], F32)
                tmp = [sb2("lntmp0", [128, 512], F32)] * 2
                xno = [sb2(f"xno{i}", [128, 512], BF16) for i in range(6)]
                rl_sb = sb2("rl_sb", [128, 512], F32)

                kva_g = [None]
                kvb_g = [None]
                xb_g = [None, None]

                dq_thunks[0]()
                dq_thunks[1]()

                def xb_load(g, sg, t, xb_):
                    for grp in range(4):
                        src = xT[grp * 512:(grp + 1) * 512,
                                 t * 512:(t + 1) * 512].rearrange(
                            "(k p) s -> p k s", p=128)
                        ins = g.dma_start(
                            out=xb_[:, grp * 4 * 512:(grp + 1) * 4 * 512]
                            .rearrange("p (k s) -> p k s", k=4), in_=src)
                        inc(ins, sg, 16)

                @block.gpsimd
                def _(g):
                    sg = gq.group(g)
                    xb_load(g, sg, 0, xb[0])
                    xb_g[0] = (sg, sg.n)
                    sg = gq.group(g)
                    xb_load(g, sg, 1, xb[1])
                    xb_g[1] = (sg, sg.n)
                    sg = gq.group(g)
                    ins = g.dma_start(
                        out=kvn_sb[:, :].rearrange("p (c o) -> p c o", o=1),
                        in_=kvnw.rearrange("(c p) o -> p c o", p=128),
                        allow_slow_non_contiguous=True)
                    inc(ins, sg, 16)
                    for grp in range(4):
                        src = kvaT[grp * 512:(grp + 1) * 512, :].rearrange(
                            "(k p) l -> p k l", p=128)
                        ins = g.dma_start(
                            out=kva_sb[:, grp * 4 * 512:(grp + 1) * 4 * 512]
                            .rearrange("p (k l) -> p k l", k=4), in_=src)
                        inc(ins, sg, 16)
                    kva_g[0] = (sg, sg.n)

                xno_g = [(0, None)] * 6
                prev2 = None
                kva_last = [0, 0]
                for t in range(8):
                    xb_ = xb[t % 2]
                    own = t < 2
                    if t >= 2:
                        @block.gpsimd
                        def _(g, t=t, xb_=xb_):
                            g.wait_ge(s_t.h, kva_last[t % 2])
                            sg = gq.group(g)
                            xb_load(g, sg, t, xb_)
                            xb_g[t % 2] = (sg, sg.n)
                    gxb, nxb = xb_g[t % 2]

                    T0, S0, V0 = s_t.n, s_s.n, s_v.n
                    sums = lambda k, T0=T0: T0 + k + 1
                    bc_n = T0 + 17
                    kva = lambda k, T0=T0: T0 + 17 + k + 1
                    lat2 = lambda m, T0=T0: T0 + 33 + m + 1
                    latbc = T0 + 38
                    sq = lambda k, S0=S0: S0 + k + 1
                    sqrt1 = S0 + 17
                    latsq = lambda m, S0=S0: S0 + 17 + m + 1
                    sqrt2 = S0 + 22
                    rl_n = S0 + 23
                    stat = V0 + 1
                    mr_n = V0 + 2
                    xn_n = lambda k, V0=V0: V0 + 2 + k + 1
                    lrc = V0 + 19
                    latn_ev = V0 + 20

                    spill = {}

                    @block.scalar
                    def _(s, t=t, own=own, xb_=xb_, gxb=gxb, nxb=nxb,
                          sums=sums, kva=kva, lat2=lat2, stat=stat,
                          xn_n=xn_n, lrc=lrc, latbc=latbc, spill=spill):
                        for k in range(KC):
                            if k == 0:
                                s.wait_ge(gxb.h, nxb)
                            if k >= 2:
                                s.wait_ge(s_t.h, sums(k - 2))
                            ins = s.activation(
                                xsq[k % 2][:, :],
                                xb_[:, k * 512:(k + 1) * 512], AF.Square)
                            inc(ins, s_s)
                        s.wait_ge(s_v.h, stat)
                        ins = s.activation(strow[:, 1536:2048],
                                           strow[:, 2560:3072], AF.Sqrt)
                        inc(ins, s_s)
                        if own:
                            for k in range(KC):
                                s.wait_ge(s_v.h, xn_n(k))
                                sg = oq.group(s)
                                ins = s.dma_start(
                                    out=xn_sc[k * 128:(k + 1) * 128,
                                              t * 512:(t + 1) * 512],
                                    in_=xno[k % 6][:, :])
                                inc(ins, sg, 16)
                                spill[k] = (sg, sg.n)
                        for m in range(4):
                            if m == 0:
                                s.wait_ge(s_t.h, kva(15))
                            if m >= 2:
                                s.wait_ge(s_t.h, lat2(m - 2))
                            ins = s.activation(lsq[m % 2][:, :],
                                               pb[4 + m][:, :], AF.Square)
                            inc(ins, s_s)
                        s.wait_ge(s_v.h, lrc)
                        ins = s.activation(strow[:, 1536:2048],
                                           strow[:, 2560:3072], AF.Sqrt)
                        inc(ins, s_s)
                        s.wait_ge(s_t.h, latbc)
                        ins = s.copy(rl_sb[:, :], pb[1][:, :])
                        inc(ins, s_s)

                    @block.tensor
                    def _(tt, t=t, xb_=xb_, gxb=gxb, nxb=nxb, sq=sq,
                          latsq=latsq, sqrt2=sqrt2, mr_n=mr_n, xn_n=xn_n,
                          stat=stat, prev2=prev2):
                        for k in range(KC):
                            if k == 0:
                                tt.wait_ge(gxb.h, nxb)
                                tt.wait_ge(s_v.h, n_consts)
                                if prev2:
                                    tt.wait_ge(s_v.h, prev2["latn_ev"])
                            tt.matmul(pb[0][0:1, :], ones128[:],
                                      xb_[:, k * 512:(k + 1) * 512],
                                      start=(k == 0), stop=(k == 15),
                                      skip_group_check=True)
                            tt.wait_ge(s_s.h, sq(k))
                            ins = tt.matmul(pb[1][0:1, :], ones128[:],
                                            xsq[k % 2][:, :],
                                            start=(k == 0), stop=(k == 15),
                                            skip_group_check=True)
                            inc(ins, s_t)
                        tt.wait_ge(s_v.h, mr_n)
                        tt.matmul(pb[2][:, :], ones1f[:],
                                  strow[:, 1536:2048], start=True, stop=True)
                        ins = tt.matmul(pb[3][:, :], ones1f[:],
                                        strow[:, 2048:2560], start=True,
                                        stop=True)
                        inc(ins, s_t)
                        for k in range(KC):
                            tt.wait_ge(s_v.h, xn_n(k))
                            if t == 0 and k == 0:
                                tt.wait_ge(kva_g[0][0].h, kva_g[0][1])
                            for m in range(4):
                                ins = tt.matmul(
                                    pb[4 + m][:, :],
                                    kva_sb[:, k * 512 + m * 128:
                                           k * 512 + (m + 1) * 128],
                                    xno[k % 6][:, :],
                                    start=(k == 0), stop=(k == 15),
                                    skip_group_check=True)
                            inc(ins, s_t)
                        for m in range(4):
                            tt.wait_ge(s_s.h, latsq(m))
                            if m == 0:
                                tt.wait_ge(s_v.h, stat)
                            ins = tt.matmul(pb[0][0:1, :], ones128[:],
                                            lsq[m % 2][:, :],
                                            start=(m == 0), stop=(m == 3),
                                            skip_group_check=True)
                            inc(ins, s_t)
                        tt.wait_ge(s_s.h, sqrt2)
                        ins = tt.matmul(pb[1][:, :], ones1f[:],
                                        strow[:, 1536:2048], start=True,
                                        stop=True)
                        inc(ins, s_t)

                    @block.vector
                    def _(v, t=t, xb_=xb_, sums=sums, sqrt1=sqrt1,
                          bc_n=bc_n, kva=kva, lat2=lat2, rl_n=rl_n,
                          spill=spill, prev2=prev2,
                          xno_g=list(xno_g)):
                        v.wait_ge(s_t.h, sums(15))
                        m_r = strow[:, 0:512]
                        msq = strow[:, 512:1024]
                        var = strow[:, 1024:1536]
                        v.tensor_scalar(m_r, pb[0][0:1, :], 1.0 / D, None,
                                        ALU.mult)
                        v.tensor_mul(msq, m_r, m_r)
                        v.scalar_tensor_tensor(var, pb[1][0:1, :], 1.0 / D,
                                               msq, ALU.mult, ALU.subtract)
                        v.tensor_scalar(var, var, 1e-5, None, ALU.add)
                        ins = v.reciprocal(strow[:, 2560:3072], var)
                        inc(ins, s_v)
                        v.wait_ge(s_s.h, sqrt1)
                        ins = v.tensor_mul(strow[:, 2048:2560],
                                           strow[:, 0:512],
                                           strow[:, 1536:2048])
                        inc(ins, s_v)
                        for k in range(KC):
                            if k == 0:
                                v.wait_ge(s_t.h, bc_n)
                            if k >= 6:
                                v.wait_ge(s_t.h, kva(k - 6))
                                pg = spill.get(k - 6)
                                if pg:
                                    v.wait_ge(pg[0].h, pg[1])
                            else:
                                gt, go = xno_g[k % 6]
                                if gt:
                                    v.wait_ge(s_t.h, gt)
                                if go:
                                    v.wait_ge(go[0].h, go[1])
                            v.tensor_mul(tmp[k % 2][:, :],
                                         xb_[:, k * 512:(k + 1) * 512],
                                         pb[2][:, :])
                            ins = v.tensor_sub(xno[k % 6][:, :],
                                               tmp[k % 2][:, :],
                                               pb[3][:, :])
                            inc(ins, s_v)
                        v.wait_ge(s_t.h, lat2(3))
                        var = strow[:, 1024:1536]
                        v.tensor_scalar(var, pb[0][0:1, :], 1.0 / LAT, None,
                                        ALU.mult)
                        v.tensor_scalar(var, var, 1e-6, None, ALU.add)
                        ins = v.reciprocal(strow[:, 2560:3072], var)
                        inc(ins, s_v)
                        v.wait_ge(s_s.h, rl_n)
                        for m in range(4):
                            ins = v.tensor_mul(
                                latn[:, m * S + t * 512:
                                     m * S + (t + 1) * 512],
                                pb[4 + m][:, :], rl_sb[:, :])
                        inc(ins, s_v)

                    for k in range(KC):
                        xno_g[k % 6] = (kva(k), spill.get(k))
                    kva_last[t % 2] = kva(15)
                    assert s_t.n == T0 + 38
                    assert s_s.n == S0 + 23
                    assert s_v.n == V0 + 20
                    prev2 = {"latn_ev": latn_ev}

                    for j in DQ_SCHED[t]:
                        dq_thunks[j]()
                    if t == 0:
                        @block.gpsimd
                        def _(g):
                            sg = gq.group(g)
                            ins = g.dma_start(
                                out=kvb_sb[:, :].rearrange(
                                    "p (c o) -> p c o", c=4),
                                in_=kvbT.rearrange("(c p) o -> p c o",
                                                   p=128))
                            inc(ins, sg, 16)
                            kvb_g[0] = (sg, sg.n)
                    if t == 1:
                        @block.vector
                        def _(v):
                            v.wait_ge(kva_g[0][0].h, kva_g[0][1])
                            v.wait_ge(kvb_g[0][0].h, kvb_g[0][1])
                            for c in range(4):
                                ins = v.tensor_scalar(
                                    kvb_sb[:, c * 4096:(c + 1) * 4096],
                                    kvb_sb[:, c * 4096:(c + 1) * 4096],
                                    kvn_sb[:, c:c + 1], None, ALU.mult)
                            inc(ins, s_v)

            phase_barrier()

            p3o = ExitStack()
            Q_all = p3o.enter_context(
                nc.sbuf_tensor("Q_all", [128, NH * QOWN], BF16))
            cs_sb = p3o.enter_context(nc.sbuf_tensor("cs_sb", [RD, S], BF16))
            sn_sb = p3o.enter_context(nc.sbuf_tensor("sn_sb", [RD, S], BF16))
            with p3o:
                p3 = ExitStack()
                with p3:
                    def sb3(n_, shape, dt):
                        return p3.enter_context(nc.sbuf_tensor(n_, shape, dt))
                    qpan = [sb3("qpan0", [128, KC * 512], BF16)] * 2
                    xnr = [sb3(f"xnr{i}", [128, QOWN], BF16)
                           for i in range(4)]
                    qcrs = sb3("qcrs", [RD, QOWN], BF16)
                    tq = [sb3(f"tq{i}", [RD, QOWN], BF16) for i in range(2)]

                    cs_g = [None]

                    @block.sync
                    def _(sy):
                        sg = yq.group(sy)
                        ins = sy.dma_start(out=cs_sb[:, :], in_=cosb[:, :])
                        inc(ins, sg, 16)
                        ins = sy.dma_start(out=sn_sb[:, :], in_=sinnb[:, :])
                        inc(ins, sg, 16)
                        cs_g[0] = (sg, sg.n)

                    st_qpan = [0, 0]
                    st_xnr = [0, 0, 0, 0]
                    q_rope = [0]
                    for mg in range(4):
                        qp = qpan[mg % 2]
                        qp_g = [None]

                        @block.gpsimd
                        def _(g, mg=mg, qp=qp, qp_g=qp_g):
                            if st_qpan[0]:
                                g.wait_ge(s_t.h, st_qpan[0])
                            sg = gq.group(g)
                            for grp in range(4):
                                src = qwT[grp * 512:(grp + 1) * 512,
                                          mg * 512:(mg + 1) * 512].rearrange(
                                    "(k p) o -> p k o", p=128)
                                ins = g.dma_start(
                                    out=qp[:, grp * 4 * 512:
                                           (grp + 1) * 4 * 512]
                                    .rearrange("p (k o) -> p k o", k=4),
                                    in_=src)
                                inc(ins, sg, 16)
                            qp_g[0] = (sg, sg.n)

                        for k in range(KC):
                            xn_ = xnr[k % 4]
                            xn_g = [None]

                            @block.gpsimd
                            def _(g, k=k, xn_=xn_, xn_g=xn_g):
                                if st_xnr[k % 4]:
                                    g.wait_ge(s_t.h, st_xnr[k % 4])
                                sg = gq.group(g)
                                ins = g.dma_start(
                                    out=xn_[:, :],
                                    in_=xn_sc[k * 128:(k + 1) * 128, :])
                                inc(ins, sg, 16)
                                xn_g[0] = (sg, sg.n)

                            @block.tensor
                            def _(tt, k=k, mg=mg, qp=qp, xn_=xn_,
                                  qp_g=qp_g, xn_g=xn_g):
                                tt.wait_ge(qp_g[0][0].h, qp_g[0][1])
                                tt.wait_ge(xn_g[0][0].h, xn_g[0][1])
                                if k == 0 and q_rope[0]:
                                    tt.wait_ge(s_v.h, q_rope[0])
                                for mi in range(4):
                                    for qc in range(2):
                                        ins = tt.matmul(
                                            pb[mi * 2 + qc][:, :],
                                            qp[:, k * 512 + mi * 128:
                                               k * 512 + (mi + 1) * 128],
                                            xn_[:, qc * 512:(qc + 1) * 512],
                                            start=(k == 0), stop=(k == 15),
                                            skip_group_check=True)
                                st_xnr[k % 4] = inc(ins, s_t)
                        n_qmm = s_t.n
                        st_qpan[0] = n_qmm

                        @block.scalar
                        def _(s, mg=mg, n_qmm=n_qmm):
                            s.wait_ge(s_t.h, n_qmm)
                            for mi in range(4):
                                h = mg * 4 + mi
                                for qc in range(2):
                                    ins = s.copy(
                                        Q_all[:, h * QOWN + qc * 512:
                                              h * QOWN + (qc + 1) * 512],
                                        pb[mi * 2 + qc][:, :])
                            inc(ins, s_s)
                        n_qev = s_s.n

                        for mi in range(4):
                            h = mg * 4 + mi

                            @block.tensor
                            def _(tt, h=h, mi=mi, n_qev=n_qev):
                                tt.wait_ge(s_s.h, n_qev)
                                if h == 0:
                                    tt.wait_ge(s32_g[0][0].h, s32_g[0][1])
                                for qc in range(2):
                                    ins = tt.matmul(
                                        pb[mi * 2 + qc][0:RD, :], s32_sb[:],
                                        Q_all[0:RD,
                                              h * QOWN + qc * 512:
                                              h * QOWN + (qc + 1) * 512],
                                        start=True, stop=True)
                                inc(ins, s_t)
                            n_qx = s_t.n

                            @block.vector
                            def _(v, h=h, mi=mi, n_qx=n_qx):
                                v.wait_ge(s_t.h, n_qx)
                                if h == 0:
                                    v.wait_ge(cs_g[0][0].h, cs_g[0][1])
                                for qc in range(2):
                                    v.tensor_copy(
                                        qcrs[:, qc * 512:(qc + 1) * 512],
                                        pb[mi * 2 + qc][0:RD, :])
                                hs = slice(h * QOWN, (h + 1) * QOWN)
                                v.tensor_mul(tq[0][:, :], Q_all[0:RD, hs],
                                             cs_sb[:, 0:QOWN])
                                v.tensor_mul(tq[1][:, :], qcrs[:, :],
                                             sn_sb[:, 0:QOWN])
                                ins = v.tensor_add(Q_all[0:RD, hs],
                                                   tq[0][:, :], tq[1][:, :])
                                inc(ins, s_v)
                            q_rope[0] = s_v.n
                    n_qrope = s_v.n

                    @block.gpsimd
                    def _(g):
                        for s_, n_ in yq.totals():
                            if n_:
                                g.wait_ge(s_.h, n_)
                        ins = g.collective_compute(
                            "AllGather", ALU.bypass,
                            replica_groups=[list(range(NCORES))],
                            ins=[dq_gu_in.opt()], outs=[dq_gu.opt()])
                        inc(ins, s_c)
                        ins = g.collective_compute(
                            "AllGather", ALU.bypass,
                            replica_groups=[list(range(NCORES))],
                            ins=[dq_dn_in.opt()], outs=[dq_dn.opt()])
                        inc(ins, s_c)
                    n_ag_done = s_c.n
                phase_barrier()

                p4 = ExitStack()
                with p4:
                    def sb4(n_, shape, dt):
                        return p4.enter_context(nc.sbuf_tensor(n_, shape, dt))
                    K_pre = sb4("K_pre", [128, S], BF16)
                    K_crs = [sb4(f"K_crs{i}", [RD, 512], BF16) for i in range(2)]
                    V_h = sb4("V_h", [128, S], BF16)
                    tk = [sb4(f"tk{i}", [RD, 512], BF16) for i in range(2)]
                    P_t = [sb4(f"P{i}", [128, 512], BF16) for i in range(3)]
                    rs_row = sb4("rs_row", [1, 1024], F32)
                    rb_sb = sb4("rb_sb", [128, 512], F32)

                    prev = None
                    for h in range(NH):
                        T0, S0, V0 = s_t.n, s_s.n, s_v.n
                        kmm = lambda n, T0=T0: T0 + n + 1
                        xmm = lambda n, T0=T0: T0 + 8 + n + 1
                        vmm = lambda g, T0=T0: T0 + 16 + g + 1
                        smm = (lambda qc, kc, T0=T0:
                               T0 + 24 + qc * 65
                               + (1 if kc == 0 else 2 * kc))
                        avs = (lambda qc, kc, T0=T0:
                               T0 + 24 + qc * 65
                               + (2 * kc + 3 if kc <= 30 else 64))
                        rbc = lambda qc, T0=T0: T0 + 24 + qc * 65 + 65
                        kev = lambda n, S0=S0: S0 + n + 1
                        expn = (lambda qc, kc, S0=S0:
                                S0 + 8 + qc * 33 + kc + 1)
                        rbcp = lambda qc, S0=S0: S0 + 8 + qc * 33 + 33
                        xev = lambda n, V0=V0: V0 + n + 1
                        rope_n = V0 + 8
                        vev = lambda g, V0=V0: V0 + 8 + g + 1
                        recip = lambda qc, V0=V0: V0 + 16 + 2 * qc + 1
                        oev = lambda qc, V0=V0: V0 + 16 + 2 * qc + 2

                        @block.tensor
                        def _(tt, h=h, T0=T0, prev=prev, kev=kev, expn=expn,
                              rbcp=rbcp, xev=xev, vev=vev, recip=recip,
                              oev=oev, rope_n=rope_n):
                            if h == 0:
                                tt.wait_ge(s_v.h, n_qrope)
                            for n in range(8):
                                if n >= 2:
                                    tt.wait_ge(s_s.h, kev(n - 2))
                                elif prev:
                                    tt.wait_ge(s_s.h, prev["kev"](6 + n))
                                for c in range(4):
                                    ins = tt.matmul(
                                        pb[n % 2][:, :],
                                        kvb_sb[:, c * 4096 + h * 256:
                                               c * 4096 + h * 256 + 128],
                                        latn[:, c * S + n * 512:
                                             c * S + (n + 1) * 512],
                                        start=(c == 0), stop=(c == 3))
                                inc(ins, s_t)
                            for n in range(8):
                                tt.wait_ge(s_s.h, kev(n))
                                if n >= 2:
                                    tt.wait_ge(s_v.h, xev(n - 2))
                                elif prev:
                                    tt.wait_ge(s_v.h, prev["vev"](6 + n))
                                ins = tt.matmul(
                                    pb[2 + n % 2][0:RD, :], s32_sb[:],
                                    K_pre[0:RD, n * 512:(n + 1) * 512],
                                    start=True, stop=True)
                                inc(ins, s_t)
                            for g8 in range(8):
                                if g8 < 2:
                                    tt.wait_ge(s_v.h, xev(6 + g8))
                                else:
                                    tt.wait_ge(s_v.h, vev(g8 - 2))
                                for j4 in range(4):
                                    j = g8 * 4 + j4
                                    for c in range(4):
                                        ins = tt.matmul(
                                            pb[2 + g8 % 2][:, j4 * 128:
                                                           (j4 + 1) * 128],
                                            latn[:, c * S + j * 128:
                                                 c * S + (j + 1) * 128],
                                            kvb_sb[:,
                                                   c * 4096 + h * 256 + 128:
                                                   c * 4096 + h * 256 + 256],
                                            start=(c == 0), stop=(c == 3),
                                            skip_group_check=True)
                                inc(ins, s_t)
                            for qc in range(2):
                                def emit_s(kc):
                                    if kc >= 2:
                                        tt.wait_ge(s_s.h, expn(qc, kc - 2))
                                    elif kc == 0:
                                        if qc == 1:
                                            tt.wait_ge(s_s.h, rbcp(0))
                                        elif prev:
                                            tt.wait_ge(s_s.h,
                                                       prev["rbcp"](1))
                                    else:
                                        if qc == 1:
                                            tt.wait_ge(s_s.h, expn(0, 31))
                                        elif prev:
                                            tt.wait_ge(s_s.h,
                                                       prev["expn"](1, 31))
                                    if kc == 0 and qc == 0:
                                        tt.wait_ge(s_v.h, rope_n)
                                    ins = tt.matmul(
                                        pb[6 + kc % 2][:, :],
                                        K_pre[:, kc * 128:(kc + 1) * 128],
                                        Q_all[:, h * QOWN + qc * 512:
                                              h * QOWN + (qc + 1) * 512],
                                        start=True, stop=True)
                                    inc(ins, s_t)

                                def emit_av(kc):
                                    tt.wait_ge(s_s.h, expn(qc, kc))
                                    if kc == 0:
                                        tt.wait_ge(s_v.h, vev(7))
                                        if qc == 1:
                                            tt.wait_ge(s_v.h, oev(0))
                                            tt.wait_ge(s_v.h, recip(0))
                                        elif prev:
                                            tt.wait_ge(s_v.h,
                                                       prev["oev"](1))
                                            tt.wait_ge(s_v.h,
                                                       prev["recip"](1))
                                    tt.matmul(
                                        pb[4][:, :],
                                        V_h[:, kc * 128:(kc + 1) * 128],
                                        P_t[kc % 3][:, :],
                                        start=(kc == 0), stop=(kc == 31),
                                        skip_group_check=True)
                                    ins = tt.matmul(
                                        pb[5][0:1, :], ones128[:],
                                        P_t[kc % 3][:, :],
                                        start=(kc == 0), stop=(kc == 31),
                                        skip_group_check=True)
                                    inc(ins, s_t)

                                for kc in range(32):
                                    emit_s(kc)
                                    if kc >= 1:
                                        emit_av(kc - 1)
                                emit_av(31)
                                tt.wait_ge(s_v.h, recip(qc))
                                ins = tt.matmul(
                                    pb[6][:, :], ones1f[:],
                                    rs_row[:, qc * 512:(qc + 1) * 512],
                                    start=True, stop=True)
                                inc(ins, s_t)

                        @block.scalar
                        def _(s, h=h, S0=S0, prev=prev, kmm=kmm, smm=smm,
                              avs=avs, rbc=rbc, oev=oev):
                            for n in range(8):
                                s.wait_ge(s_t.h, kmm(n))
                                if n == 0 and prev:
                                    s.wait_ge(s_t.h, prev["avs"](1, 31))
                                ins = s.copy(
                                    K_pre[:, n * 512:(n + 1) * 512],
                                    pb[n % 2][:, :])
                                inc(ins, s_s)
                            for qc in range(2):
                                for kc in range(32):
                                    s.wait_ge(s_t.h, smm(qc, kc))
                                    if kc >= 3:
                                        s.wait_ge(s_t.h, avs(qc, kc - 3))
                                    elif qc == 1:
                                        s.wait_ge(s_t.h, avs(0, 29 + kc))
                                    elif prev:
                                        s.wait_ge(s_t.h,
                                                  prev["avs"](1, 29 + kc))
                                    ins = s.activation(P_t[kc % 3][:, :],
                                                       pb[6 + kc % 2][:, :],
                                                       AF.Exp, scale=SMAX)
                                    inc(ins, s_s)
                                s.wait_ge(s_t.h, rbc(qc))
                                if qc == 1:
                                    s.wait_ge(s_v.h, oev(0))
                                elif prev:
                                    s.wait_ge(s_v.h, prev["oev"](1))
                                ins = s.copy(rb_sb[:, :], pb[6][:, :])
                                inc(ins, s_s)

                        @block.vector
                        def _(v, h=h, V0=V0, prev=prev, xmm=xmm, vmm=vmm,
                              avs=avs, rbcp=rbcp):
                            for n in range(8):
                                v.wait_ge(s_t.h, xmm(n))
                                c0 = n * 512
                                v.tensor_copy(K_crs[n % 2][:, :],
                                              pb[2 + n % 2][0:RD, :])
                                v.tensor_mul(tk[0][:, :],
                                             K_pre[0:RD, c0:c0 + 512],
                                             cs_sb[:, c0:c0 + 512])
                                v.tensor_mul(tk[1][:, :],
                                             K_crs[n % 2][:, :],
                                             sn_sb[:, c0:c0 + 512])
                                ins = v.tensor_add(
                                    K_pre[0:RD, c0:c0 + 512],
                                    tk[0][:, :], tk[1][:, :])
                                inc(ins, s_v)
                            for g8 in range(8):
                                v.wait_ge(s_t.h, vmm(g8))
                                if g8 == 0 and prev:
                                    v.wait_ge(s_t.h, prev["avs"](1, 31))
                                ins = v.tensor_copy(
                                    V_h[:, g8 * 512:(g8 + 1) * 512],
                                    pb[2 + g8 % 2][:, :])
                                inc(ins, s_v)
                            for qc in range(2):
                                v.wait_ge(s_t.h, avs(qc, 31))
                                ins = v.reciprocal(
                                    rs_row[:, qc * 512:(qc + 1) * 512],
                                    pb[5][0:1, :])
                                inc(ins, s_v)
                                v.wait_ge(s_s.h, rbcp(qc))
                                ins = v.tensor_mul(
                                    O_all[:, h * QOWN + qc * 512:
                                          h * QOWN + (qc + 1) * 512],
                                    pb[4][:, :], rb_sb[:, :])
                                inc(ins, s_v)

                        assert s_t.n == T0 + 154, (s_t.n, T0)
                        assert s_s.n == S0 + 74, (s_s.n, S0)
                        assert s_v.n == V0 + 20, (s_v.n, V0)
                        prev = {"kev": kev, "expn": expn, "rbcp": rbcp,
                                "vev": vev, "recip": recip, "oev": oev,
                                "avs": avs}
                    n_heads = s_v.n
                phase_barrier()

            p5 = ExitStack()
            with p5:
                def sb5(n_, shape, dt):
                    return p5.enter_context(nc.sbuf_tensor(n_, shape, dt))
                opan = [sb5(f"opan{i}", [128, KC * 512], BF16)
                        for i in range(2)]
                xfo = [sb5(f"xfo{i}", [128, 4 * 1024], F32) for i in range(2)]
                axst = [sb5(f"axst{i}", [128, 512], F32) for i in range(4)]

                st_opan = [0, 0]
                st_xfo = [0, 0]
                st_axst = [None] * 4
                o_ev = [0]
                for mg in range(4):
                    op = opan[mg % 2]
                    xf = xfo[mg % 2]
                    op_g = [None]
                    xf_g = [None]

                    @block.gpsimd
                    def _(g, mg=mg, op=op, op_g=op_g):
                        if st_opan[mg % 2]:
                            g.wait_ge(s_t.h, st_opan[mg % 2])
                        sg = gq.group(g)
                        for grp in range(4):
                            src = owT[grp * 512:(grp + 1) * 512,
                                      mg * 512:(mg + 1) * 512].rearrange(
                                "(k p) o -> p k o", p=128)
                            ins = g.dma_start(
                                out=op[:, grp * 4 * 512:(grp + 1) * 4 * 512]
                                .rearrange("p (k o) -> p k o", k=4),
                                in_=src)
                            inc(ins, sg, 16)
                        op_g[0] = (sg, sg.n)

                    @block.scalar
                    def _(s, mg=mg, xf=xf, xf_g=xf_g):
                        if st_xfo[mg % 2]:
                            s.wait_ge(s_v.h, st_xfo[mg % 2])
                        sg = oq.group(s)
                        src = xT[mg * 512:(mg + 1) * 512, 0:QOWN].rearrange(
                            "(f p) s -> p f s", p=128)
                        ins = s.dma_start(
                            out=xf[:, :].rearrange("p (f s) -> p f s", f=4),
                            in_=src)
                        inc(ins, sg, 16)
                        xf_g[0] = (sg, sg.n)

                    @block.tensor
                    def _(tt, mg=mg, op=op, op_g=op_g):
                        tt.wait_ge(op_g[0][0].h, op_g[0][1])
                        if mg == 0:
                            tt.wait_ge(s_v.h, n_heads)
                        if o_ev[0]:
                            tt.wait_ge(s_v.h, o_ev[0])
                        for k in range(KC):
                            for mi in range(4):
                                for qc in range(2):
                                    ins = tt.matmul(
                                        pb[mi * 2 + qc][:, :],
                                        op[:, k * 512 + mi * 128:
                                           k * 512 + (mi + 1) * 128],
                                        O_all[:, k * QOWN + qc * 512:
                                              k * QOWN + (qc + 1) * 512],
                                        start=(k == 0), stop=(k == 15),
                                        skip_group_check=True)
                        inc(ins, s_t)
                    n_omm = s_t.n
                    st_opan[mg % 2] = n_omm

                    for mi in range(4):
                        for qc in range(2):
                            sl = (mi * 2 + qc) % 4

                            @block.vector
                            def _(v, mi=mi, qc=qc, sl=sl, xf=xf,
                                  n_omm=n_omm, xf_g=xf_g):
                                v.wait_ge(s_t.h, n_omm)
                                v.wait_ge(xf_g[0][0].h, xf_g[0][1])
                                if st_axst[sl]:
                                    v.wait_ge(st_axst[sl][0].h,
                                              st_axst[sl][1])
                                ins = v.tensor_add(
                                    axst[sl][:, :], pb[mi * 2 + qc][:, :],
                                    xf[:, mi * 1024 + qc * 512:
                                       mi * 1024 + (qc + 1) * 512])
                                inc(ins, s_v)
                            n_axv = s_v.n
                            st_xfo[mg % 2] = n_axv
                            if mi == 3 and qc == 1:
                                o_ev[0] = n_axv

                            @block.scalar
                            def _(s, mg=mg, mi=mi, qc=qc, sl=sl,
                                  n_axv=n_axv):
                                s.wait_ge(s_v.h, n_axv)
                                sg = oq.group(s)
                                f = mg * 4 + mi
                                ins = s.dma_start(
                                    out=ax_sc[f * 128:(f + 1) * 128,
                                              qc * 512:(qc + 1) * 512],
                                    in_=axst[sl][:, :])
                                inc(ins, sg, 16)
                                st_axst[sl] = (sg, sg.n)
            n_p5_ev = o_ev[0]
        phase_barrier()
        p6o = ExitStack()
        h_sb = p6o.enter_context(
            nc.sbuf_tensor("h_sb", [128, DN_KC * QOWN], BF16))
        with p6o:
            p6 = ExitStack()
            with p6:
                def sb6(n_, shape, dt):
                    return p6.enter_context(nc.sbuf_tensor(n_, shape, dt))
                xn_sb = sb6("xn_sb", [128, KC * QOWN], BF16)
                wg = [sb6(f"wg{i}", [128, 2048], BF16) for i in range(2)]
                wu = [sb6(f"wu{i}", [128, 2048], BF16) for i in range(2)]
                su = [sb6(f"su{i}", [128, 512], BF16) for i in range(2)]

                T0a, S0a, V0a = s_t.n, s_s.n, s_v.n
                gu_n = lambda g: T0a + g + 1
                sil = lambda g, qc: S0a + 2 * g + qc + 1
                hm = lambda g, qc: V0a + 2 * g + qc + 1

                xnl_g = [None]
                wgrp = []

                @block.sync
                def _(sy):
                    sy.wait_ge(s_c.h, n_ag_done)
                    for s_, n_ in oq.totals():
                        if n_:
                            sy.wait_ge(s_.h, n_)
                    sg = yq.group(sy)
                    for grp in range(4):
                        src = xn_sc[grp * 512:(grp + 1) * 512, :].rearrange(
                            "(k p) s -> p k s", p=128)
                        ins = sy.dma_start(
                            out=xn_sb[:, grp * 4 * QOWN:(grp + 1) * 4 * QOWN]
                            .rearrange("p (k s) -> p k s", k=4), in_=src)
                        inc(ins, sg, 16)
                    xnl_g[0] = (sg, sg.n)
                    for g_ in range(DN_KC):
                        rg = (g_ // 11) * GU_OC + (g_ % 11)
                        ru = (4 + g_ // 11) * GU_OC + (g_ % 11)
                        if g_ >= 2:
                            sy.wait_ge(s_t.h, gu_n(g_ - 2))
                        sg = yq.group(sy)
                        ins = sy.dma_start(
                            out=wg[g_ % 2][:, :],
                            in_=dq_gu[rg * 128:(rg + 1) * 128, :])
                        inc(ins, sg, 16)
                        ins = sy.dma_start(
                            out=wu[g_ % 2][:, :],
                            in_=dq_gu[ru * 128:(ru + 1) * 128, :])
                        inc(ins, sg, 16)
                        wgrp.append((sg, sg.n))

                @block.tensor
                def _(tt):
                    tt.wait_ge(xnl_g[0][0].h, xnl_g[0][1])
                    tt.wait_ge(s_v.h, n_p5_ev)
                    for g_ in range(DN_KC):
                        tt.wait_ge(wgrp[g_][0].h, wgrp[g_][1])
                        if g_ >= 2:
                            tt.wait_ge(s_v.h, hm(g_ - 2, 1))
                        bb = (g_ % 2) * 4
                        for k in range(KC):
                            for qc in range(2):
                                tt.matmul(
                                    pb[bb + qc][:, :],
                                    wg[g_ % 2][:, k * 128:(k + 1) * 128],
                                    xn_sb[:, k * QOWN + qc * 512:
                                          k * QOWN + (qc + 1) * 512],
                                    start=(k == 0), stop=(k == 15),
                                    skip_group_check=True)
                                ins = tt.matmul(
                                    pb[bb + 2 + qc][:, :],
                                    wu[g_ % 2][:, k * 128:(k + 1) * 128],
                                    xn_sb[:, k * QOWN + qc * 512:
                                          k * QOWN + (qc + 1) * 512],
                                    start=(k == 0), stop=(k == 15),
                                    skip_group_check=True)
                        inc(ins, s_t)

                @block.scalar
                def _(s):
                    for g_ in range(DN_KC):
                        bb = (g_ % 2) * 4
                        for qc in range(2):
                            s.wait_ge(s_t.h, gu_n(g_))
                            if g_ >= 1:
                                s.wait_ge(s_v.h, hm(g_ - 1, qc))
                            ins = s.activation(su[qc][:, :],
                                               pb[bb + qc][:, :], AF.Silu)
                            inc(ins, s_s)

                @block.vector
                def _(v):
                    for g_ in range(DN_KC):
                        bb = (g_ % 2) * 4
                        for qc in range(2):
                            v.wait_ge(s_s.h, sil(g_, qc))
                            ins = v.tensor_mul(
                                h_sb[:, g_ * QOWN + qc * 512:
                                     g_ * QOWN + (qc + 1) * 512],
                                su[qc][:, :], pb[bb + 2 + qc][:, :])
                            inc(ins, s_v)

                assert s_t.n == T0a + DN_KC
                assert s_s.n == S0a + 2 * DN_KC
                assert s_v.n == V0a + 2 * DN_KC
                n_h_last = s_v.n
            phase_barrier()

            p6 = ExitStack()
            with p6:
                def sb6(n_, shape, dt):
                    return p6.enter_context(nc.sbuf_tensor(n_, shape, dt))
                dn_sb = sb6("dn_sb", [128, 16 * 512], F32)
                wd = [sb6(f"wd{i}", [128, DN_KC * 128], BF16)
                      for i in range(2)]
                dsq = [sb6(f"dsq{i}", [128, 512], BF16) for i in range(2)]
                ry_sb = sb6("ry_sb", [128, 1024], F32)
                axr = [sb6(f"axr{i}", [128, 512], F32) for i in range(2)]
                ost = [sb6(f"ost{i}", [128, 512], F32) for i in range(2)]
                drow = sb6("drow", [1, 1024], F32)

                prevb = None
                for tb in range(2):
                    T0, S0, V0 = s_t.n, s_s.n, s_v.n
                    dn_n = lambda fc, T0=T0: T0 + (1 if fc == 0 else 2 * fc)
                    s2_n = (lambda fc, T0=T0:
                            T0 + (2 * fc + 3 if fc <= 14 else 32))
                    dbc_n = T0 + 33
                    dnev = lambda fc, S0=S0: S0 + fc + 1
                    dri_n = S0 + 17
                    ry_n = S0 + 18
                    vr_n = V0 + 1
                    o1 = lambda fc, V0=V0: V0 + fc + 2

                    wdg = []
                    axg = []
                    ostg = []

                    @block.sync
                    def _(sy, tb=tb, dn_n=dn_n, prevb=prevb):
                        for fc in range(16):
                            rr = (fc // 2) * DN_OC + (fc % 2)
                            if fc >= 2:
                                sy.wait_ge(s_t.h, dn_n(fc - 2))
                            elif prevb:
                                sy.wait_ge(s_t.h, prevb["dn_n"](14 + fc))
                            sg = yq.group(sy)
                            ins = sy.dma_start(
                                out=wd[fc % 2][:, :],
                                in_=dq_dn[rr * 128:(rr + 1) * 128, :])
                            inc(ins, sg, 16)
                            wdg.append((sg, sg.n))

                    @block.tensor
                    def _(tt, tb=tb, dnev=dnev, dri_n=dri_n, prevb=prevb):
                        def emit_dn(fc):
                            tt.wait_ge(wdg[fc][0].h, wdg[fc][1])
                            if fc >= 2:
                                tt.wait_ge(s_s.h, dnev(fc - 2))
                            elif prevb:
                                tt.wait_ge(s_s.h, prevb["dnev"](14 + fc))
                            elif fc == 0:
                                tt.wait_ge(s_v.h, n_h_last)
                            for k in range(DN_KC):
                                ins = tt.matmul(
                                    pb[(fc % 2) * 2][:, :],
                                    wd[fc % 2][:, k * 128:(k + 1) * 128],
                                    h_sb[:, k * QOWN + tb * 512:
                                         k * QOWN + (tb + 1) * 512],
                                    start=(k == 0), stop=(k == DN_KC - 1))
                            inc(ins, s_t)

                        def emit_s2(fc):
                            tt.wait_ge(s_s.h, dnev(fc))
                            if fc == 0 and prevb:
                                tt.wait_ge(s_v.h, prevb["vr_n"])
                            ins = tt.matmul(pb[4][0:1, :], ones128[:],
                                            dsq[fc % 2][:, :],
                                            start=(fc == 0), stop=(fc == 15),
                                            skip_group_check=True)
                            inc(ins, s_t)

                        for fc in range(16):
                            emit_dn(fc)
                            if fc >= 1:
                                emit_s2(fc - 1)
                        emit_s2(15)
                        tt.wait_ge(s_s.h, dri_n)
                        ins = tt.matmul(pb[5][:, :], ones1f[:],
                                        drow[:, 0:512], start=True,
                                        stop=True)
                        inc(ins, s_t)

                    @block.scalar
                    def _(s, tb=tb, dn_n=dn_n, s2_n=s2_n, dbc_n=dbc_n,
                          vr_n=vr_n, o1=o1, prevb=prevb):
                        for fc in range(16):
                            s.wait_ge(s_t.h, dn_n(fc))
                            if fc >= 2:
                                s.wait_ge(s_t.h, s2_n(fc - 2))
                            elif prevb:
                                s.wait_ge(s_t.h, prevb["s2_n"](14 + fc))
                            if prevb:
                                s.wait_ge(s_v.h, prevb["o1"](fc))
                            s.copy(dn_sb[:, fc * 512:(fc + 1) * 512],
                                   pb[(fc % 2) * 2][:, :])
                            ins = s.activation(dsq[fc % 2][:, :],
                                               pb[(fc % 2) * 2][:, :],
                                               AF.Square)
                            inc(ins, s_s)
                        s.wait_ge(s_v.h, vr_n)
                        ins = s.activation(drow[:, 0:512],
                                           drow[:, 512:1024], AF.Sqrt)
                        inc(ins, s_s)
                        s.wait_ge(s_t.h, dbc_n)
                        ins = s.copy(ry_sb[:, tb * 512:(tb + 1) * 512],
                                     pb[5][:, :])
                        inc(ins, s_s)
                        for fc in range(16):
                            if fc >= 2:
                                s.wait_ge(s_v.h, o1(fc - 2))
                            elif prevb:
                                s.wait_ge(s_v.h, prevb["o1"](14 + fc))
                            sg = oq.group(s)
                            ins = s.dma_start(
                                out=axr[fc % 2][:, :],
                                in_=ax_sc[fc * 128:(fc + 1) * 128,
                                          tb * 512:(tb + 1) * 512])
                            inc(ins, sg, 16)
                            axg.append((sg, sg.n))
                            s.wait_ge(s_v.h, o1(fc))
                            sg = oq.group(s)
                            ins = s.dma_start(
                                out=yT[fc * 128:(fc + 1) * 128,
                                       tb * 512:(tb + 1) * 512],
                                in_=ost[fc % 2][:, :])
                            inc(ins, sg, 16)
                            ostg.append((sg, sg.n))

                    @block.vector
                    def _(v, tb=tb, s2_n=s2_n, ry_n=ry_n, prevb=prevb):
                        v.wait_ge(s_t.h, s2_n(15))
                        r0 = drow[:, 0:512]
                        v.tensor_scalar(r0, pb[4][0:1, :], 1.0 / D, None,
                                        ALU.mult)
                        v.tensor_scalar(r0, r0, 1e-6, None, ALU.add)
                        ins = v.reciprocal(drow[:, 512:1024], r0)
                        inc(ins, s_v)
                        for fc in range(16):
                            if fc == 0:
                                v.wait_ge(s_s.h, ry_n)
                            v.wait_ge(axg[fc][0].h, axg[fc][1])
                            if fc >= 2:
                                v.wait_ge(ostg[fc - 2][0].h,
                                          ostg[fc - 2][1])
                            elif prevb:
                                pg = prevb["ostg"][14 + fc]
                                v.wait_ge(pg[0].h, pg[1])
                            v.tensor_mul(ost[fc % 2][:, :],
                                         dn_sb[:, fc * 512:(fc + 1) * 512],
                                         ry_sb[:, tb * 512:
                                               (tb + 1) * 512])
                            ins = v.tensor_add(ost[fc % 2][:, :],
                                               ost[fc % 2][:, :],
                                               axr[fc % 2][:, :])
                            inc(ins, s_v)

                    assert s_t.n == T0 + 33
                    assert s_s.n == S0 + 18
                    assert s_v.n == V0 + 17
                    prevb = {"dn_n": dn_n, "dnev": dnev, "s2_n": s2_n,
                             "vr_n": vr_n, "o1": o1, "ostg": ostg}
        if DEBUG:
            @block.gpsimd
            def _(g):
                g.wait_ge(s_c.h, n_ag_done)
                sg = gq.group(g)
                ins = g.dma_start(out=dbg_gu, in_=dq_gu)
                inc(ins, sg, 16)
                ins = g.dma_start(out=dbg_dn, in_=dq_dn)
                inc(ins, sg, 16)

        @block.sync
        def _(sy):
            for s_, n_ in (gq.totals() + yq.totals() + oq.totals()):
                if n_:
                    sy.wait_ge(s_.h, n_)

        p1.close()

    return nc



def _perm(n_feat):
    p = np.empty(n_feat, np.int64)
    ar = np.arange(128)
    for t in range(n_feat // 256):
        p[256 * t:256 * t + 128] = 256 * t + 2 * ar
        p[256 * t + 128:256 * t + 256] = 256 * t + 2 * ar + 1
    return p


def _yarn_tables():
    ROPE_DIM, MAX_POS, ORIG_POS, BASE = 64, 131072, 8192, 500000.0
    BETA_FAST, BETA_SLOW = 32.0, 1.0
    scaling = MAX_POS / ORIG_POS
    pos_freqs = BASE ** (np.arange(0, ROPE_DIM, 2, dtype=np.float64)
                         / ROPE_DIM)
    extrap = 1.0 / pos_freqs
    interp = 1.0 / (scaling * pos_freqs)
    low = max(math.floor(ROPE_DIM
                         * math.log(ORIG_POS / (BETA_FAST * 2 * math.pi))
                         / (2 * math.log(BASE))), 0)
    high = min(math.ceil(ROPE_DIM
                         * math.log(ORIG_POS / (BETA_SLOW * 2 * math.pi))
                         / (2 * math.log(BASE))), ROPE_DIM - 1)
    i = np.arange(ROPE_DIM // 2, dtype=np.float64)
    smooth = np.clip((i - low) / max(high - low, 1), 0.0, 1.0)
    inv_freq = ((1.0 - smooth) * interp + smooth * extrap).astype(np.float32)
    pos = np.arange(S, dtype=np.float32)
    freqs = pos[:, None] * inv_freq[None, :]
    emb = np.concatenate([freqs, freqs], -1)
    mscale = 0.1 * math.log(scaling) + 1.0
    cos = (np.cos(emb) * mscale).astype(np.float32).T
    sin = (np.sin(emb) * mscale).astype(np.float32).T
    sinneg = sin.copy()
    sinneg[0:32] = -sin[0:32]
    return cos, sinneg


def kernel(L0_attn_hidden_states, L0_attn_q_w, L0_attn_kv_a_w, L0_attn_kv_b_w,
           L0_attn_o_w, L0_attn_kv_norm_w, L0_ffn_gate_up_packed,
           L0_ffn_gate_up_scales, L0_ffn_down_packed, L0_ffn_down_scales):
    x = np.asarray(L0_attn_hidden_states, np.float32)
    q_w = np.asarray(L0_attn_q_w, np.float32)
    kv_a = np.asarray(L0_attn_kv_a_w, np.float32)
    kv_b = np.asarray(L0_attn_kv_b_w, np.float32)
    o_w = np.asarray(L0_attn_o_w, np.float32)
    kv_n = np.asarray(L0_attn_kv_norm_w, np.float32)
    gup = np.asarray(L0_ffn_gate_up_packed, np.int32)
    gusc = np.asarray(L0_ffn_gate_up_scales)
    dnp = np.asarray(L0_ffn_down_packed, np.int32)
    dnsc = np.asarray(L0_ffn_down_scales)

    P_D = _perm(D)
    P_I = _perm(IFF)

    qwT = np.ascontiguousarray(q_w.T[P_D])
    kvaT = np.ascontiguousarray(kv_a[RD:, :].T[P_D])
    kvbT = np.ascontiguousarray(kv_b.T)
    owT = np.ascontiguousarray(o_w.T[:, P_D])
    kvnw = np.ascontiguousarray(kv_n.reshape(LAT, 1))
    cos, sinneg = _yarn_tables()
    s32 = np.zeros((RD, RD), np.float32)
    s32[(np.arange(RD) + 32) % RD, np.arange(RD)] = 1.0
    s32 = s32.astype(ml_dtypes.bfloat16)

    gu_stack = np.concatenate([gup[:IFF][P_I], gup[IFF:][P_I]], 0)
    gubT = np.ascontiguousarray(gu_stack.view(np.uint8).T)
    gus_stack = np.concatenate(
        [gusc[:, :IFF][:, P_I], gusc[:, IFF:][:, P_I]], 1)
    dnbT = np.ascontiguousarray(dnp[P_D].view(np.uint8).T)
    dns_perm = np.ascontiguousarray(dnsc[:, P_D])

    if "nc" not in _CACHE:
        _CACHE["nc"] = build_program()
    nc = _CACHE["nc"]

    in_maps = []
    for c in range(NCORES):
        b, r = c // 4, c % 4
        xTc = np.roll(x[b].T[P_D], -QOWN * r, axis=1)
        in_maps.append({
            "xT": np.ascontiguousarray(xTc),
            "qwT": qwT,
            "kvaT": kvaT,
            "kvbT": kvbT,
            "kvnw": kvnw,
            "owT": owT,
            "cosb": np.ascontiguousarray(
                np.roll(cos, -QOWN * r, axis=1)).astype(ml_dtypes.bfloat16),
            "sinnb": np.ascontiguousarray(
                np.roll(sinneg, -QOWN * r, axis=1)).astype(
                    ml_dtypes.bfloat16),
            "s32i": s32,
            "gub": np.ascontiguousarray(gubT[:, GU_NO * c:GU_NO * (c + 1)]),
            "gus": np.ascontiguousarray(
                gus_stack[:, GU_NO * c:GU_NO * (c + 1)]),
            "dnb": np.ascontiguousarray(dnbT[:, DN_NO * c:DN_NO * (c + 1)]),
            "dns": np.ascontiguousarray(
                dns_perm[:, DN_NO * c:DN_NO * (c + 1)]),
        })

    res = run_bass_kernel_spmd(nc, in_maps, core_ids=list(range(NCORES)))

    out = np.empty((B, S, D), np.float32)
    for c in range(NCORES):
        b, r = c // 4, c % 4
        yTc = np.asarray(res.results[c]["yT"])
        blk = np.empty((QOWN, D), np.float32)
        blk[:, P_D] = yTc.T
        out[b, QOWN * r:QOWN * (r + 1), :] = blk
    if DEBUG:
        _CACHE["debug"] = res
    return out
